# version 05
# baseline (speedup 1.0000x reference)
import numpy as np
import ml_dtypes
from contextlib import ExitStack
import concourse.bass as bass
import concourse.mybir as mybir
from concourse.bass_utils import run_bass_kernel_spmd
from concourse.bass_core_rust import IndirectOffsetOnAxis

F32 = mybir.dt.float32
BF16 = mybir.dt.bfloat16
U32 = mybir.dt.uint32
AF = mybir.ActivationFunctionType
ALU = mybir.AluOpType
AX = mybir.AxisListType

P = 128
NCORES = 8
H = 2048
HC = H // P
I = 768
IC = I // P
NQ = 4
E = 32
K = 8
C = 768
ST = C // P
NSUB = 2
TSUB = C // NSUB
S_TOT = E * C
NC_TOK = 16
NT = NC_TOK * P
EPS = 1e-5
INV_H = 1.0 / H

SUBS = ("f", "a", "f")


def build_nc():
    nc = bass.Bass("TRN2", target_bir_lowering=False)
    dram = nc.dram_tensor
    x_in = dram("x_in", [NT, H], F32, kind="ExternalInput").ap()
    wg_t = dram("wg_t", [E, IC, P, HC * 128], F32, kind="ExternalInput").ap()
    wu_t = dram("wu_t", [E, IC, P, HC * 128], F32, kind="ExternalInput").ap()
    wd_t = dram("wd_t", [E, NQ, P, IC * 512], F32, kind="ExternalInput").ap()
    wTt = dram("wTt", [P, HC * H], F32, kind="ExternalInput").ap()
    dwTt = dram("dwTt", [P, HC * H], F32, kind="ExternalInput").ap()
    rwTt = dram("rwTt", [P, HC * E], F32, kind="ExternalInput").ap()
    idn_in = dram("idn_in", [P, P], BF16, kind="ExternalInput").ap()
    lt_in = dram("lt_in", [P, 2 * P], BF16, kind="ExternalInput").ap()
    ecc_in = dram("ecc_in", [P, E], F32, kind="ExternalInput").ap()
    y_out = dram("y", [NT, H], F32, kind="ExternalOutput").ap()
    stream = dram("stream", [NT, H], F32, kind="Internal").ap()
    xslots = dram("xslots", [S_TOT, H], BF16, kind="Internal").ap()
    dall = dram("dall", [S_TOT, H], BF16, kind="Internal").ap()

    es = ExitStack()
    with es:
        sb = lambda name, shape, dt: es.enter_context(nc.sbuf_tensor(name, shape, dt))
        pst = lambda name, shape, dt: es.enter_context(nc.psum_tensor(name, shape, dt))
        big = sb("big", [P, 38912], BF16)
        xg = [big[:, 0:12288], big[:, 12288:24576]]
        h_sl = big[:, 24576:29184]
        o_sl = [big[:, 29184:31232], big[:, 31232:33280]]
        comb = [big[:, 33280:35328], big[:, 35328:37376]]
        wsumT = big[:, 0:32768]

        wg_r = sb("wg_r", [P, 3 * 2048], BF16)
        wu_r = sb("wu_r", [P, 3 * 2048], BF16)
        wd_r = sb("wd_r", [P, 6 * 3072], BF16)
        xsl = sb("xsl", [P, 2 * H], BF16)
        x_tm = sb("x_tm", [P, 2 * H], F32)
        stage = sb("stage", [P, 2 * H], BF16)
        x_fm_r = sb("x_fm_r", [P, 2 * H], BF16)
        o_t = sb("o_t", [P, 2 * H], F32)
        sg = sb("sg", [P, 2 * 512], F32)
        lg = sb("lg", [P, E], F32)
        w_sb = sb("w_sb", [P, E], F32)
        mask_f = sb("mask_f", [P, E], F32)
        mask_b = sb("mask_b", [P, NC_TOK * E], BF16)
        pos_sb = sb("pos_sb", [P, E], F32)
        excl = sb("excl", [P, E], F32)
        slotid = sb("slotid", [P, E], F32)
        eqk = sb("eqk", [P, E], F32)
        selk = sb("selk", [P, E], F32)
        tmpk = sb("tmpk", [P, E], F32)
        ones32 = sb("ones32", [P, E], F32)
        incl32 = sb("incl32", [P, E], F32)
        sk_f = sb("sk_f", [P, 1], F32)
        sk_u = sb("sk_u", [P, NC_TOK * K], U32)
        g_k = sb("g_k", [P, NC_TOK * K], F32)
        esb = sb("esb", [P, E], F32)
        zap = sb("zap", [P, E], F32)
        wsel = sb("wsel", [P, E], F32)
        m8 = sb("m8", [P, 8], F32)
        rwT_sb = sb("rwT_sb", [P, HC * E], BF16)
        idn = sb("idn", [P, P], BF16)
        ltb = sb("ltb", [P, 2 * P], BF16)
        ecc = sb("ecc", [P, E], F32)
        st_s = sb("st_s", [P, NC_TOK], F32)
        st_q = sb("st_q", [P, NC_TOK], F32)
        st_mean = sb("st_mean", [P, NC_TOK], F32)
        st_m2 = sb("st_m2", [P, NC_TOK], F32)
        st_var = sb("st_var", [P, NC_TOK], F32)
        st_ln = sb("st_ln", [P, NC_TOK], F32)
        st_rstd = sb("st_rstd", [P, NC_TOK], F32)
        st_negmr = sb("st_negmr", [P, NC_TOK], F32)
        st_nmax = sb("st_nmax", [P, NC_TOK], F32)
        st_rs = sb("st_rs", [P, NC_TOK], F32)
        st_rr = sb("st_rr", [P, NC_TOK], F32)
        eps_t = sb("eps_t", [P, 1], F32)

        pg = [pst("pg0", [P, 512], F32), pst("pg1", [P, 512], F32)]
        pu = [pst("pu0", [P, 512], F32), pst("pu1", [P, 512], F32)]
        pd = [pst("pd0", [P, 512], F32), pst("pd1", [P, 512], F32)]
        pt = [pst("pt0", [P, 1024], BF16), pst("pt1", [P, 1024], BF16)]

        sem_names = (["SP", "SV", "SA", "SG", "GI", "SI", "SO", "XT0", "XT1",
                      "DS", "DG", "SD", "XS"]
                     + [f"WG{i}" for i in range(3)] + [f"WU{i}" for i in range(3)]
                     + [f"WD{i}" for i in range(6)])
        sems = {n: es.enter_context(nc.semaphore(n)) for n in sem_names}
        block = es.enter_context(nc.Block())

        sched = []
        cnt = {n: 0 for n in sem_names}
        _self_sem = {"vector": "SV", "scalar": "SA"}

        def op(engine, fn, waits=(), inc=None):
            best = {}
            ss = _self_sem.get(engine)
            if ss is None and engine == "gpsimd" and inc is not None and inc[0] == "SG":
                ss = "SG"
            waits = list(waits) + ([(ss, cnt[ss])] if ss else [])
            for w in waits:
                if w is None:
                    continue
                s, v = w
                if v > 0 and v > best.get(s, 0):
                    best[s] = v
            incs = []
            if inc is not None:
                cnt[inc[0]] += inc[1]
                incs.append(inc)
            sched.append((engine, fn, list(best.items()), incs))

        def mark(sem):
            return (sem, cnt[sem])

        wg_war = [None] * 3
        wu_war = [None] * 3
        wd_war = [None] * 6
        wg_rdy = [None] * 3
        wu_rdy = [None] * 3
        wd_rdy = [None] * 6
        xt_war = [[], []]
        xt_rdy = [None, None]
        pg_war = [None, None]
        pu_war = [None, None]
        pd_war = [None, None]
        pt_war = [None, None]
        sg_war = [None, None]
        stage_war = [[], []]
        xfm_war = [[], []]
        xsl_war = [[], []]
        xg_war = [[], []]
        osl_war = [[], []]
        comb_war = [[], []]
        acc_war = [[], []]
        h_war = [None]
        o_war = [None, None]
        esb_war = [None]
        big_attn_war = [[]]
        state = {"gi": 0, "di": 0, "pti": 0, "xslslot": 0,
                 "ds_all": None, "sd_all": None, "dg_all": None, "xs_all": None}
        store_marks = [None] * NC_TOK

        op("sync", lambda e: e.dma_start(out=idn[:], in_=idn_in), inc=("SI", 16))
        op("sync", lambda e: e.dma_start(out=ltb[:], in_=lt_in), inc=("SI", 16))
        op("sync", lambda e: e.dma_start(out=ecc[:], in_=ecc_in), inc=("SI", 16))
        op("gpsimd", lambda g: g.dma_start(out=rwT_sb[:], in_=rwTt), inc=("GI", 16))
        op("vector", lambda v: v.memset(eps_t[:], EPS), inc=("SV", 1))
        op("vector", lambda v: v.memset(ones32[:], 1.0), inc=("SV", 1))
        idn_rdy = mark("SI")
        rw_rdy = mark("GI")
        eps_rdy = ("SV", 1)
        ones_rdy = ("SV", 2)

        def load_wg(e_, m, seq):
            slot = seq % 3
            op("gpsimd", (lambda sl, ee, mm: lambda s: s.dma_start(
                out=wg_r[:, sl * 2048:(sl + 1) * 2048], in_=wg_t[ee, mm]))(slot, e_, m),
               waits=[wg_war[slot]], inc=(f"WG{slot}", 16))
            wg_rdy[slot] = mark(f"WG{slot}")

        def load_wu(e_, m, seq):
            slot = seq % 3
            op("gpsimd", (lambda sl, ee, mm: lambda s: s.dma_start(
                out=wu_r[:, sl * 2048:(sl + 1) * 2048], in_=wu_t[ee, mm]))(slot, e_, m),
               waits=[wu_war[slot]], inc=(f"WU{slot}", 16))
            wu_rdy[slot] = mark(f"WU{slot}")

        def load_wd(e_, q, seq):
            slot = seq % 6
            op("gpsimd", (lambda sl, ee, qq: lambda s: s.dma_start(
                out=wd_r[:, sl * 3072:(sl + 1) * 3072], in_=wd_t[ee, qq]))(slot, e_, q),
               waits=[wd_war[slot]], inc=(f"WD{slot}", 16))
            wd_rdy[slot] = mark(f"WD{slot}")

        def ln_chunk(src, tc):
            slot = tc % 2
            stg_v = stage[:, slot * H:(slot + 1) * H]
            op("sync", (lambda sl, r: lambda e: e.dma_start(
                out=x_tm[:, sl * H:(sl + 1) * H], in_=src[r:r + P]))(slot, tc * P),
               waits=xt_war[slot] + [store_marks[tc]], inc=(f"XT{slot}", 16))
            xt_rdy[slot] = mark(f"XT{slot}")
            op("vector", (lambda sl, t: lambda v: v.reduce_sum(
                st_s[:, t:t + 1], x_tm[:, sl * H:(sl + 1) * H], axis=AX.X))(slot, tc),
               waits=[xt_rdy[slot]], inc=("SV", 1))
            s_rdy = mark("SV")
            op("scalar", (lambda sl, t, sv: lambda s: s.activation(
                sv, x_tm[:, sl * H:(sl + 1) * H], AF.Square,
                accum_out=st_q[:, t:t + 1]))(slot, tc, stg_v),
               waits=[xt_rdy[slot]] + stage_war[slot], inc=("SA", 1))
            q_rdy = mark("SA")
            op("vector", (lambda t: lambda v: v.tensor_scalar_mul(
                st_mean[:, t:t + 1], st_s[:, t:t + 1], INV_H))(tc),
               waits=[s_rdy], inc=("SV", 1))
            op("vector", (lambda t: lambda v: v.tensor_mul(
                st_m2[:, t:t + 1], st_mean[:, t:t + 1], st_mean[:, t:t + 1]))(tc),
               waits=[mark("SV")], inc=("SV", 1))
            op("vector", (lambda t: lambda v: v.scalar_tensor_tensor(
                out=st_var[:, t:t + 1], in0=st_q[:, t:t + 1], scalar=INV_H,
                in1=st_m2[:, t:t + 1], op0=ALU.mult, op1=ALU.subtract))(tc),
               waits=[q_rdy, mark("SV")], inc=("SV", 1))
            var_rdy = mark("SV")
            op("scalar", (lambda t: lambda s: s.activation(
                st_ln[:, t:t + 1], st_var[:, t:t + 1], AF.Ln, bias=eps_t[:]))(tc),
               waits=[var_rdy, eps_rdy], inc=("SA", 1))
            op("scalar", (lambda t: lambda s: s.activation(
                st_rstd[:, t:t + 1], st_ln[:, t:t + 1], AF.Exp, scale=-0.5))(tc),
               waits=[mark("SA")], inc=("SA", 1))
            rstd_rdy = mark("SA")
            op("vector", (lambda t: lambda v: v.scalar_tensor_tensor(
                out=st_negmr[:, t:t + 1], in0=st_mean[:, t:t + 1], scalar=-1.0,
                in1=st_rstd[:, t:t + 1], op0=ALU.mult, op1=ALU.mult))(tc),
               waits=[rstd_rdy, mark("SV")], inc=("SV", 1))
            nm_rdy = mark("SV")
            op("scalar", (lambda sl, t, sv: lambda s: s.activation(
                sv, x_tm[:, sl * H:(sl + 1) * H], AF.Identity,
                bias=st_negmr[:, t:t + 1], scale=st_rstd[:, t:t + 1]))(slot, tc, stg_v),
               waits=[nm_rdy, mark("SA")], inc=("SA", 1))
            norm_rdy = mark("SA")
            xt_war[slot] = [norm_rdy, s_rdy]
            return norm_rdy

        def fm_chunk(tc, norm_rdy):
            slot = tc % 2
            evs = []
            for g_ in range(2):
                pti = state["pti"]
                for j in range(8):
                    c = 8 * g_ + j
                    waits = []
                    if j == 0:
                        waits = [norm_rdy, pt_war[pti], idn_rdy]
                    inc = ("SP", 1) if j == 7 else None
                    op("tensor", (lambda pi, sl, cc, jj: lambda t: t.transpose(
                        pt[pi][:, jj * 128:(jj + 1) * 128],
                        stage[:, sl * H + cc * 128: sl * H + (cc + 1) * 128],
                        idn[:]))(pti, slot, c, j),
                       waits=waits, inc=inc)
                tr_done = mark("SP")
                op("vector", (lambda pi, sl, g2: lambda v: v.tensor_copy(
                    x_fm_r[:, sl * H + g2 * 1024: sl * H + (g2 + 1) * 1024],
                    pt[pi][:, 0:1024]))(pti, slot, g_),
                   waits=[tr_done] + xfm_war[slot], inc=("SV", 1))
                pt_war[pti] = mark("SV")
                evs.append(mark("SV"))
                state["pti"] ^= 1
            stage_war[slot] = [tr_done]
            xfm_war[slot] = []
            return evs[-1], tr_done

        def route_chunk(tc, fm_rdy, sub_base):
            slot = tc % 2
            for c in range(HC):
                waits = []
                if c == 0:
                    waits = [fm_rdy, rw_rdy, pg_war[0]]
                inc = ("SP", 1) if c == HC - 1 else None
                op("tensor", (lambda sl, cc: lambda te: te.matmul(
                    pg[0][:, 0:E],
                    x_fm_r[:, sl * H + cc * 128: sl * H + (cc + 1) * 128],
                    rwT_sb[:, cc * E:(cc + 1) * E],
                    start=(cc == 0), stop=(cc == HC - 1)))(slot, c),
                   waits=waits, inc=inc)
            lg_rdy = mark("SP")
            xfm_war[slot].append(lg_rdy)
            op("vector", lambda v: v.tensor_copy(lg[:], pg[0][:, 0:E]),
               waits=[lg_rdy], inc=("SV", 1))
            pg_war[0] = mark("SV")
            op("vector", (lambda t: lambda v: v.reduce_max(
                st_nmax[:, t:t + 1], lg[:], axis=AX.X, negate=True))(tc),
               waits=[mark("SV"), mark("SA")], inc=("SV", 1))
            op("scalar", (lambda t: lambda s: s.activation(
                esb[:], lg[:], AF.Exp, bias=st_nmax[:, t:t + 1]))(tc),
               waits=[mark("SV"), esb_war[0]], inc=("SA", 1))
            e_rdy = mark("SA")
            op("vector", lambda v: v.max(out=m8[:], in_=esb[:]),
               waits=[e_rdy, mark("SV")], inc=("SV", 1))
            op("vector", lambda v: v.match_replace(
                out=zap[:], in_to_replace=m8[:], in_values=esb[:], imm_value=0.0),
               waits=[mark("SV")], inc=("SV", 1))
            op("vector", lambda v: v.tensor_sub(wsel[:], esb[:], zap[:]),
               waits=[mark("SV")], inc=("SV", 1))
            esb_war[0] = mark("SV")
            op("vector", (lambda t: lambda v: v.reduce_sum(
                st_rs[:, t:t + 1], wsel[:], axis=AX.X))(tc),
               waits=[mark("SV")], inc=("SV", 1))
            op("vector", (lambda t: lambda v: v.reciprocal(
                st_rr[:, t:t + 1], st_rs[:, t:t + 1]))(tc),
               waits=[mark("SV")], inc=("SV", 1))
            op("vector", (lambda t: lambda v: v.tensor_scalar_mul(
                w_sb[:], wsel[:], st_rr[:, t:t + 1]))(tc),
               waits=[mark("SV")], inc=("SV", 1))
            op("vector", lambda v: v.tensor_scalar(
                out=mask_f[:], in0=w_sb[:], scalar1=0.0, scalar2=None, op0=ALU.is_gt),
               waits=[mark("SV")], inc=("SV", 1))
            mf_rdy = mark("SV")
            op("vector", (lambda t: lambda v: v.tensor_copy(
                mask_b[:, t * E:(t + 1) * E], mask_f[:]))(tc),
               waits=[mf_rdy], inc=("SV", 1))
            mb_rdy = mark("SV")
            op("vector", lambda v: v.tensor_tensor_scan(
                out=incl32[:], data0=ones32[:], data1=mask_f[:],
                initial=0.0, op0=ALU.mult, op1=ALU.add),
               waits=[mf_rdy, ones_rdy], inc=("SV", 1))
            op("vector", lambda v: v.tensor_sub(excl[:], incl32[:], mask_f[:]),
               waits=[mark("SV")], inc=("SV", 1))
            excl_rdy = mark("SV")
            for i in range(tc + 1):
                waits = []
                if i == 0:
                    waits = [mb_rdy, pg_war[1], idn_rdy]
                inc = ("SP", 1) if i == tc else None
                lsl = (0 if i == tc else P)
                op("tensor", (lambda ii, ls: lambda te: te.matmul(
                    pg[1][:, 0:E],
                    ltb[:, ls:ls + P],
                    mask_b[:, ii * E:(ii + 1) * E],
                    start=(ii == 0), stop=(ii == tc)))(i, lsl),
                   waits=waits, inc=inc)
            pos_rdy = mark("SP")
            op("vector", lambda v: v.tensor_copy(pos_sb[:], pg[1][:, 0:E]),
               waits=[pos_rdy], inc=("SV", 1))
            pg_war[1] = mark("SV")
            op("vector", lambda v: v.tensor_add(slotid[:], pos_sb[:], ecc[:]),
               waits=[mark("SV")], inc=("SV", 1))
            sid_rdy = mark("SV")
            for k in range(K):
                op("vector", (lambda kk: lambda v: v.tensor_scalar(
                    out=eqk[:], in0=excl[:], scalar1=float(kk),
                    scalar2=None, op0=ALU.is_equal))(k),
                   waits=[excl_rdy, mark("SV")], inc=("SV", 1))
                op("vector", lambda v: v.tensor_mul(selk[:], eqk[:], mask_f[:]),
                   waits=[mark("SV")], inc=("SV", 1))
                op("vector", lambda v: v.tensor_mul(tmpk[:], selk[:], slotid[:]),
                   waits=[sid_rdy, mark("SV")], inc=("SV", 1))
                op("vector", lambda v: v.reduce_sum(sk_f[:], tmpk[:], axis=AX.X),
                   waits=[mark("SV")], inc=("SV", 1))
                op("vector", (lambda t, kk: lambda v: v.tensor_copy(
                    sk_u[:, t * K + kk: t * K + kk + 1], sk_f[:]))(tc, k),
                   waits=[mark("SV")], inc=("SV", 1))
                sku_rdy = mark("SV")
                op("vector", lambda v: v.tensor_mul(tmpk[:], selk[:], w_sb[:]),
                   waits=[mark("SV")], inc=("SV", 1))
                op("vector", (lambda t, kk: lambda v: v.reduce_sum(
                    g_k[:, t * K + kk: t * K + kk + 1], tmpk[:], axis=AX.X))(tc, k),
                   waits=[mark("SV")], inc=("SV", 1))
                op("gpsimd", (lambda sl, t, kk, sbv: lambda g: g.indirect_dma_start(
                    out=xslots,
                    out_offset=IndirectOffsetOnAxis(
                        sk_u[:, t * K + kk: t * K + kk + 1], 0),
                    in_=sbv, in_offset=None))(
                        slot, tc, k, stage[:, slot * H:(slot + 1) * H]),
                   waits=[sku_rdy, sub_base], inc=("DS", 16))
                stage_war[slot] = stage_war[slot] + [mark("DS")]

        def emit_tile(e_, st, rdy, ds_all):
            xb = e_ % 2
            xslslot = state["xslslot"]
            state["xslslot"] ^= 1
            op("sync", (lambda ee, ss, xsl_s: lambda e: e.dma_start(
                out=xsl[:, xsl_s * H:(xsl_s + 1) * H],
                in_=xslots[ee * C + ss * P: ee * C + ss * P + P]))(e_, st, xslslot),
               waits=xsl_war[xslslot] + [ds_all], inc=("XS", 16))
            xsl_rdy = mark("XS")
            tr_done = None
            for g_ in range(2):
                pti = state["pti"]
                for j in range(8):
                    c = 8 * g_ + j
                    waits = []
                    if j == 0:
                        waits = [xsl_rdy, pt_war[pti], idn_rdy]
                    inc = ("SP", 1) if j == 7 else None
                    op("tensor", (lambda pi, xs, cc, jj: lambda t: t.transpose(
                        pt[pi][:, jj * 128:(jj + 1) * 128],
                        xsl[:, xs * H + cc * 128: xs * H + (cc + 1) * 128],
                        idn[:]))(pti, xslslot, c, j),
                       waits=waits, inc=inc)
                tr_done = mark("SP")
                op("vector", (lambda pi, xv, g2, ss: lambda v: v.tensor_copy(
                    xv.rearrange("p (c s) -> p c s", c=HC)[
                        :, g2 * 8:(g2 + 1) * 8, ss * P:(ss + 1) * P],
                    pt[pi][:, 0:1024].rearrange("p (b t) -> p b t", b=8)))(
                        pti, xg[xb], g_, st),
                   waits=[tr_done] + xg_war[xb], inc=("SV", 1))
                pt_war[pti] = mark("SV")
                state["pti"] ^= 1
            xg_war[xb] = []
            xsl_war[xslslot] = [tr_done]
            rdy.append(mark("SV"))

        def experts(sweep):
            ds_all = mark("DS")
            load_wg(0, 0, sweep * 7919 + 0)
            load_wu(0, 0, sweep * 7919 + 0)
            load_wg(0, 1, sweep * 7919 + 1)
            load_wu(0, 1, sweep * 7919 + 1)
            load_wd(0, 0, sweep * 7919 + 0)
            tiles_rdy = []
            for st in range(ST):
                emit_tile(0, st, tiles_rdy, ds_all)
            for e_ in range(E):
                xb = e_ % 2
                tile_rdy = tiles_rdy
                h_last = None
                for m in range(IC):
                    seq = sweep * 7919 + e_ * IC + m
                    slot = seq % 3
                    wgr = wg_rdy[slot]
                    wur = wu_rdy[slot]
                    g_done = None
                    u_done = None
                    for n in range(NSUB):
                        xg_rdy = tile_rdy[3 * n + 2]
                        gi = state["gi"]
                        g_b, u_b = pg[gi], pu[gi]
                        for c in range(HC):
                            waits = []
                            if c == 0:
                                waits = [wgr, xg_rdy, pg_war[gi]]
                            inc = ("SP", 1) if c == HC - 1 else None
                            op("tensor", (lambda gb, sl, cc, nn, xv: lambda t: t.matmul(
                                gb[:, 0:TSUB],
                                wg_r[:, sl * 2048 + cc * 128: sl * 2048 + cc * 128 + 128],
                                xv[:, cc * C + nn * TSUB: cc * C + (nn + 1) * TSUB],
                                start=(cc == 0), stop=(cc == HC - 1)))(g_b, slot, c, n, xg[xb]),
                               waits=waits, inc=inc)
                        g_done = mark("SP")
                        for c in range(HC):
                            waits = []
                            if c == 0:
                                waits = [wur, xg_rdy, pu_war[gi]]
                            inc = ("SP", 1) if c == HC - 1 else None
                            op("tensor", (lambda ub, sl, cc, nn, xv: lambda t: t.matmul(
                                ub[:, 0:TSUB],
                                wu_r[:, sl * 2048 + cc * 128: sl * 2048 + cc * 128 + 128],
                                xv[:, cc * C + nn * TSUB: cc * C + (nn + 1) * TSUB],
                                start=(cc == 0), stop=(cc == HC - 1)))(u_b, slot, c, n, xg[xb]),
                               waits=waits, inc=inc)
                        u_done = mark("SP")
                        sgv = sg[:, gi * 512: gi * 512 + TSUB]
                        op("scalar", (lambda gb, sv: lambda s: s.activation(
                            sv, gb[:, 0:TSUB], AF.Silu))(g_b, sgv),
                           waits=[g_done, sg_war[gi]], inc=("SA", 1))
                        pg_war[gi] = mark("SA")
                        silu_done = mark("SA")
                        hv = h_sl[:, m * C + n * TSUB: m * C + (n + 1) * TSUB]
                        op("vector", (lambda ub, sv, hh: lambda v: v.tensor_mul(
                            hh, sv, ub[:, 0:TSUB]))(u_b, sgv, hv),
                           waits=[u_done, silu_done, h_war[0]], inc=("SV", 1))
                        pu_war[gi] = mark("SV")
                        sg_war[gi] = mark("SV")
                        h_last = mark("SV")
                        state["gi"] ^= 1
                    wg_war[slot] = g_done
                    wu_war[slot] = u_done
                    nm, ne_ = m + 2, e_
                    if nm >= IC:
                        nm -= IC
                        ne_ += 1
                    if ne_ < E:
                        load_wg(ne_, nm, sweep * 7919 + ne_ * IC + nm)
                        load_wu(ne_, nm, sweep * 7919 + ne_ * IC + nm)
                xg_war[xb] = [mark("SP")]
                big_attn_war[0] = big_attn_war[0] + [mark("SP")]
                tiles_rdy = []
                for st in range(ST):
                    if e_ + 1 < E:
                        emit_tile(e_ + 1, st, tiles_rdy, ds_all)
                    osl = st % 2
                    q_done = []
                    for q in range(NQ):
                        seq = sweep * 7919 + e_ * NQ + q
                        slot = seq % 6
                        wdr = wd_rdy[slot]
                        if st == 0:
                            nq_, ne_ = q + 1, e_
                            if nq_ >= NQ:
                                nq_ -= NQ
                                ne_ += 1
                            if ne_ < E:
                                load_wd(ne_, nq_, sweep * 7919 + ne_ * NQ + nq_)
                        di = state["di"]
                        d_b = pd[di]
                        for c in range(IC):
                            waits = []
                            if c == 0:
                                waits = [wdr, h_last, pd_war[di]]
                            inc = ("SP", 1) if c == IC - 1 else None
                            op("tensor", (lambda db, sl, cc, qq, ss: lambda t: t.matmul(
                                db[:],
                                h_sl[:, cc * C + ss * P: cc * C + (ss + 1) * P],
                                wd_r[:, sl * 3072 + cc * 512: sl * 3072 + (cc + 1) * 512],
                                start=(cc == 0), stop=(cc == IC - 1)))(d_b, slot, c, q, st),
                               waits=waits, inc=inc)
                        d_done = mark("SP")
                        if st == ST - 1:
                            wd_war[slot] = d_done
                        op("vector", (lambda db, ov, qq: lambda v: v.tensor_copy(
                            ov[:, qq * 512:(qq + 1) * 512], db[:]))(d_b, o_sl[osl], q),
                           waits=[d_done] + osl_war[osl], inc=("SV", 1))
                        pd_war[di] = mark("SV")
                        q_done.append(mark("SV"))
                        state["di"] ^= 1
                    osl_war[osl] = []
                    op("sync", (lambda ov, ee, ss: lambda e: e.dma_start(
                        out=dall[ee * C + ss * P: ee * C + ss * P + P], in_=ov))(
                            o_sl[osl], e_, st),
                       waits=[q_done[-1], state["dg_all"]], inc=("SD", 16))
                    osl_war[osl] = [mark("SD")]
                h_war[0] = mark("SP")
                big_attn_war[0] = big_attn_war[0] + [h_war[0]]
            state["sd_all"] = mark("SD")
            state["xs_all"] = mark("XS")

        def combine(src, dst, scale):
            sd_all = state["sd_all"]
            for tc in range(NC_TOK):
                slot = tc % 2
                av = o_t[:, slot * H:(slot + 1) * H]
                op("sync", (lambda sl, r: lambda e: e.dma_start(
                    out=x_tm[:, sl * H:(sl + 1) * H], in_=src[r:r + P]))(slot, tc * P),
                   waits=xt_war[slot] + [store_marks[tc]], inc=(f"XT{slot}", 16))
                xt_rdy[slot] = mark(f"XT{slot}")
                last_acc = None
                for k in range(K):
                    cb = k % 2
                    op("gpsimd", (lambda t, kk, cv: lambda g: g.indirect_dma_start(
                        out=cv, out_offset=None,
                        in_=dall,
                        in_offset=IndirectOffsetOnAxis(
                            sk_u[:, t * K + kk: t * K + kk + 1], 0)))(tc, k, comb[cb]),
                       waits=[sd_all] + comb_war[cb], inc=("DG", 16))
                    g_rdy = mark("DG")
                    gv = g_k[:, tc * K + k: tc * K + k + 1]
                    if k == 0:
                        op("vector", (lambda cv, aa, gg: lambda v: v.tensor_scalar(
                            out=aa, in0=cv, scalar1=gg, scalar2=None,
                            op0=ALU.mult))(comb[cb], av, gv),
                           waits=[g_rdy, o_war[slot]], inc=("SV", 1))
                    else:
                        op("vector", (lambda cv, aa, gg: lambda v: v.scalar_tensor_tensor(
                            out=aa, in0=cv, scalar=gg, in1=aa,
                            op0=ALU.mult, op1=ALU.add))(comb[cb], av, gv),
                           waits=[g_rdy], inc=("SV", 1))
                    comb_war[cb] = [mark("SV")]
                    last_acc = mark("SV")
                op("vector", (lambda sl, aa, sc: lambda v: v.scalar_tensor_tensor(
                    out=aa, in0=aa, scalar=sc,
                    in1=x_tm[:, sl * H:(sl + 1) * H], op0=ALU.mult, op1=ALU.add))(
                        slot, av, scale),
                   waits=[xt_rdy[slot], last_acc], inc=("SV", 1))
                xt_war[slot] = [mark("SV")]
                stt_done = mark("SV")
                op("sync", (lambda sl, r, aa: lambda e: e.dma_start(
                    out=dst[r:r + P], in_=aa))(slot, tc * P, av),
                   waits=[stt_done], inc=("SO", 16))
                o_war[slot] = mark("SO")
                store_marks[tc] = mark("SO")
            state["dg_all"] = mark("DG")

        def ffn_sublayer(src, dst, sweep, scale=0.5, post_expert_hook=None):
            sub_base = state["xs_all"]
            for tc in range(NC_TOK):
                norm_rdy = ln_chunk(src, tc)
                fm_rdy, _ = fm_chunk(tc, norm_rdy)
                route_chunk(tc, fm_rdy, sub_base)
            experts(sweep)
            if post_expert_hook is not None:
                post_expert_hook()
            combine(src, dst, scale)

        def attn_prep():
            last_add = None
            for ci in range(HC):
                slot = ci % 3
                op("gpsimd", (lambda sl, c: lambda s: s.dma_start(
                    out=wg_r[:, sl * 2048:(sl + 1) * 2048],
                    in_=wTt[:, c * H:(c + 1) * H]))(slot, ci),
                   waits=[wg_war[slot]], inc=(f"WG{slot}", 16))
                wg_rdy[slot] = mark(f"WG{slot}")
                op("gpsimd", (lambda sl, c: lambda s: s.dma_start(
                    out=wu_r[:, sl * 2048:(sl + 1) * 2048],
                    in_=dwTt[:, c * H:(c + 1) * H]))(slot, ci),
                   waits=[wu_war[slot]], inc=(f"WU{slot}", 16))
                wu_rdy[slot] = mark(f"WU{slot}")
                op("gpsimd", (lambda sl, c: lambda g: g.tensor_add(
                    wsumT[:, c * H:(c + 1) * H],
                    wg_r[:, sl * 2048:(sl + 1) * 2048],
                    wu_r[:, sl * 2048:(sl + 1) * 2048]))(slot, ci),
                   waits=[wg_rdy[slot], wu_rdy[slot], state["sd_all"], h_war[0]]
                         + big_attn_war[0] + xg_war[0] + xg_war[1],
                   inc=("SG", 1))
                wg_war[slot] = mark("SG")
                wu_war[slot] = mark("SG")
                last_add = mark("SG")
            big_attn_war[0] = []
            state["wsum_rdy"] = last_add

        def attn_sublayer():
            wsum_rdy = state["wsum_rdy"]
            for tc in range(NC_TOK):
                slot = tc % 2
                norm_rdy = ln_chunk(stream, tc)
                fm_rdy, _ = fm_chunk(tc, norm_rdy)
                for no in range(NQ):
                    di = state["di"]
                    d_b = pd[di]
                    for ci in range(HC):
                        waits = []
                        if ci == 0:
                            waits = [wsum_rdy, fm_rdy, pd_war[di]]
                        inc = ("SP", 1) if ci == HC - 1 else None
                        op("tensor", (lambda db, sl, cc, nn: lambda te: te.matmul(
                            db[:],
                            x_fm_r[:, sl * H + cc * 128: sl * H + (cc + 1) * 128],
                            wsumT[:, cc * H + nn * 512: cc * H + (nn + 1) * 512],
                            start=(cc == 0), stop=(cc == HC - 1)))(d_b, slot, ci, no),
                           waits=waits, inc=inc)
                    d_done = mark("SP")
                    if no == NQ - 1:
                        xfm_war[slot].append(d_done)
                    op("vector", (lambda db, sl, nn: lambda v: v.tensor_add(
                        o_t[:, sl * H + nn * 512: sl * H + (nn + 1) * 512], db[:],
                        x_tm[:, sl * H + nn * 512: sl * H + (nn + 1) * 512]))(d_b, slot, no),
                       waits=[d_done, o_war[slot]], inc=("SV", 1))
                    pd_war[di] = mark("SV")
                    state["di"] ^= 1
                ev_done = mark("SV")
                xt_war[slot] = xt_war[slot] + [ev_done]
                op("sync", (lambda sl, r: lambda e: e.dma_start(
                    out=stream[r:r + P], in_=o_t[:, sl * H:(sl + 1) * H]))(slot, tc * P),
                   waits=[ev_done], inc=("SO", 16))
                o_war[slot] = mark("SO")
                store_marks[tc] = mark("SO")
            m = mark("SP")
            xg_war[0] = xg_war[0] + [m]
            xg_war[1] = xg_war[1] + [m]
            h_war[0] = m

        sweep = 0
        for si, kind in enumerate(SUBS):
            src = x_in if si == 0 else stream
            dst = y_out if si == len(SUBS) - 1 else stream
            if kind == "f":
                nxt_attn = si + 1 < len(SUBS) and SUBS[si + 1] == "a"
                ffn_sublayer(src, dst, sweep,
                             post_expert_hook=attn_prep if nxt_attn else None)
                sweep += 1
            else:
                attn_sublayer()

        total_so = cnt["SO"]
        op("sync", lambda e: None, waits=[("SO", total_so)])

        def emit(engine_name):
            def run(eng):
                waited = {}
                for (en, fn, ws, incs) in sched:
                    if en != engine_name:
                        continue
                    for (s, v) in ws:
                        if v > waited.get(s, 0):
                            eng.wait_ge(sems[s], v)
                            waited[s] = v
                    ins = fn(eng)
                    for (s, n) in incs:
                        ins.then_inc(sems[s], n)
            return run

        block.vector(emit("vector"))
        block.scalar(emit("scalar"))
        block.tensor(emit("tensor"))
        block.gpsimd(emit("gpsimd"))
        block.sync(emit("sync"))
    return nc


def _tile_inputs(L0_attn_W, L0_attn_delta_W, L0_ffn_router_w, L0_ffn_w_gate,
                 L0_ffn_w_up, L0_ffn_w_down):
    f32 = np.float32
    wg = np.asarray(L0_ffn_w_gate, dtype=f32)
    wu = np.asarray(L0_ffn_w_up, dtype=f32)
    wd = np.asarray(L0_ffn_w_down, dtype=f32)
    wg_t = np.ascontiguousarray(
        wg.reshape(E, HC, P, IC, 128).transpose(0, 3, 2, 1, 4).reshape(E, IC, P, HC * 128))
    wu_t = np.ascontiguousarray(
        wu.reshape(E, HC, P, IC, 128).transpose(0, 3, 2, 1, 4).reshape(E, IC, P, HC * 128))
    wd_t = np.ascontiguousarray(
        wd.reshape(E, IC, P, NQ, 512).transpose(0, 3, 2, 1, 4).reshape(E, NQ, P, IC * 512))
    WT = np.ascontiguousarray(np.asarray(L0_attn_W, dtype=f32).T)
    dWT = np.ascontiguousarray(np.asarray(L0_attn_delta_W, dtype=f32).T)
    wTt = np.ascontiguousarray(WT.reshape(HC, P, H).transpose(1, 0, 2).reshape(P, HC * H))
    dwTt = np.ascontiguousarray(dWT.reshape(HC, P, H).transpose(1, 0, 2).reshape(P, HC * H))
    rwT = np.ascontiguousarray(np.asarray(L0_ffn_router_w, dtype=f32).T)
    rwTt = np.ascontiguousarray(rwT.reshape(HC, P, E).transpose(1, 0, 2).reshape(P, HC * E))
    idn = np.eye(P, dtype=np.float32).astype(ml_dtypes.bfloat16)
    lt = np.triu(np.ones((P, P), dtype=np.float32), 1)
    lt_in = np.concatenate([lt, np.ones((P, P), dtype=np.float32)],
                           axis=1).astype(ml_dtypes.bfloat16)
    ecc_in = np.broadcast_to(
        (np.arange(E, dtype=np.float32) * C)[None, :], (P, E)).copy()
    return dict(wg_t=wg_t, wu_t=wu_t, wd_t=wd_t, wTt=wTt, dwTt=dwTt,
                rwTt=rwTt, idn_in=idn, lt_in=lt_in, ecc_in=ecc_in)


def kernel(L0_attn_x, L0_attn_W, L0_attn_target_W, L0_attn_delta_W,
           L0_ffn_router_w, L0_ffn_w_gate, L0_ffn_w_up, L0_ffn_w_down):
    assert np.asarray(L0_attn_target_W).shape == (H, H)
    x = np.ascontiguousarray(np.asarray(L0_attn_x, dtype=np.float32).reshape(16384, H))
    shared = _tile_inputs(L0_attn_W, L0_attn_delta_W, L0_ffn_router_w,
                          L0_ffn_w_gate, L0_ffn_w_up, L0_ffn_w_down)
    nc = build_nc()
    in_maps = [dict(shared, x_in=np.ascontiguousarray(x[c * NT:(c + 1) * NT]))
               for c in range(NCORES)]
    res = run_bass_kernel_spmd(nc, in_maps, core_ids=list(range(NCORES)))
    y = np.concatenate([np.asarray(res.results[c]["y"]) for c in range(NCORES)], axis=0)
    return y.reshape(4, 4096, H).astype(np.float32)


# version 06
# speedup vs baseline: 1.0019x; 1.0019x over previous
import numpy as np
import ml_dtypes
from contextlib import ExitStack
import concourse.bass as bass
import concourse.mybir as mybir
from concourse.bass_utils import run_bass_kernel_spmd
from concourse.bass_core_rust import IndirectOffsetOnAxis

F32 = mybir.dt.float32
BF16 = mybir.dt.bfloat16
U32 = mybir.dt.uint32
AF = mybir.ActivationFunctionType
ALU = mybir.AluOpType
AX = mybir.AxisListType

P = 128
NCORES = 8
H = 2048
HC = H // P
I = 768
IC = I // P
NQ = 4
E = 32
K = 8
C = 768
ST = C // P
NSUB = 2
TSUB = C // NSUB
S_TOT = E * C
NC_TOK = 16
NT = NC_TOK * P
EPS = 1e-5
INV_H = 1.0 / H

SUBS = ("f", "a", "f")


def build_nc():
    nc = bass.Bass("TRN2", target_bir_lowering=False)
    dram = nc.dram_tensor
    x_in = dram("x_in", [NT, H], F32, kind="ExternalInput").ap()
    wg_t = dram("wg_t", [E, IC, P, HC * 128], F32, kind="ExternalInput").ap()
    wu_t = dram("wu_t", [E, IC, P, HC * 128], F32, kind="ExternalInput").ap()
    wd_t = dram("wd_t", [E, NQ, P, IC * 512], F32, kind="ExternalInput").ap()
    wTt = dram("wTt", [P, HC * H], F32, kind="ExternalInput").ap()
    dwTt = dram("dwTt", [P, HC * H], F32, kind="ExternalInput").ap()
    rwTt = dram("rwTt", [P, HC * E], F32, kind="ExternalInput").ap()
    idn_in = dram("idn_in", [P, P], BF16, kind="ExternalInput").ap()
    lt_in = dram("lt_in", [P, 2 * P], BF16, kind="ExternalInput").ap()
    ecc_in = dram("ecc_in", [P, E], F32, kind="ExternalInput").ap()
    y_out = dram("y", [NT, H], F32, kind="ExternalOutput").ap()
    stream = dram("stream", [NT, H], F32, kind="Internal").ap()
    xslots = dram("xslots", [S_TOT, H], BF16, kind="Internal").ap()
    dall = dram("dall", [S_TOT, H], BF16, kind="Internal").ap()

    es = ExitStack()
    with es:
        sb = lambda name, shape, dt: es.enter_context(nc.sbuf_tensor(name, shape, dt))
        pst = lambda name, shape, dt: es.enter_context(nc.psum_tensor(name, shape, dt))
        big = sb("big", [P, 38912], BF16)
        xg = [big[:, 0:12288], big[:, 12288:24576]]
        h_sl = big[:, 24576:29184]
        o_sl = [big[:, 29184:31232], big[:, 31232:33280]]
        comb = [big[:, 33280:35328], big[:, 35328:37376]]
        wsumT = big[:, 0:32768]

        wg_r = sb("wg_r", [P, 3 * 2048], BF16)
        wu_r = sb("wu_r", [P, 3 * 2048], BF16)
        wd_r = sb("wd_r", [P, 6 * 3072], BF16)
        xsl = sb("xsl", [P, 2 * H], BF16)
        x_tm = sb("x_tm", [P, 2 * H], F32)
        stage = sb("stage", [P, 2 * H], BF16)
        x_fm_r = sb("x_fm_r", [P, 2 * H], BF16)
        o_t = sb("o_t", [P, 2 * H], F32)
        sg = sb("sg", [P, 2 * 512], F32)
        lg = sb("lg", [P, E], F32)
        w_sb = sb("w_sb", [P, E], F32)
        mask_f = sb("mask_f", [P, E], F32)
        mask_b = sb("mask_b", [P, NC_TOK * E], BF16)
        pos_sb = sb("pos_sb", [P, E], F32)
        excl = sb("excl", [P, E], F32)
        slotid = sb("slotid", [P, E], F32)
        eqk = sb("eqk", [P, E], F32)
        selk = sb("selk", [P, E], F32)
        tmpk = sb("tmpk", [P, E], F32)
        ones32 = sb("ones32", [P, E], F32)
        incl32 = sb("incl32", [P, E], F32)
        sk_f = sb("sk_f", [P, 1], F32)
        sk_u = sb("sk_u", [P, NC_TOK * K], U32)
        g_k = sb("g_k", [P, NC_TOK * K], F32)
        esb = sb("esb", [P, E], F32)
        zap = sb("zap", [P, E], F32)
        wsel = sb("wsel", [P, E], F32)
        m8 = sb("m8", [P, 8], F32)
        rwT_sb = sb("rwT_sb", [P, HC * E], BF16)
        idn = sb("idn", [P, P], BF16)
        ltb = sb("ltb", [P, 2 * P], BF16)
        ecc = sb("ecc", [P, E], F32)
        st_s = sb("st_s", [P, NC_TOK], F32)
        st_q = sb("st_q", [P, NC_TOK], F32)
        st_mean = sb("st_mean", [P, NC_TOK], F32)
        st_m2 = sb("st_m2", [P, NC_TOK], F32)
        st_var = sb("st_var", [P, NC_TOK], F32)
        st_ln = sb("st_ln", [P, NC_TOK], F32)
        st_rstd = sb("st_rstd", [P, NC_TOK], F32)
        st_negmr = sb("st_negmr", [P, NC_TOK], F32)
        st_nmax = sb("st_nmax", [P, NC_TOK], F32)
        st_rs = sb("st_rs", [P, NC_TOK], F32)
        st_rr = sb("st_rr", [P, NC_TOK], F32)
        eps_t = sb("eps_t", [P, 1], F32)

        pg = [pst("pg0", [P, 512], F32), pst("pg1", [P, 512], F32)]
        pu = [pst("pu0", [P, 512], F32), pst("pu1", [P, 512], F32)]
        pd = [pst("pd0", [P, 512], F32), pst("pd1", [P, 512], F32)]
        pt = [pst("pt0", [P, 1024], BF16), pst("pt1", [P, 1024], BF16)]

        sem_names = (["SP", "SV", "SA", "SG", "GI", "SI", "SO", "XT0", "XT1",
                      "DS", "DG", "SD", "XS"]
                     + [f"WG{i}" for i in range(3)] + [f"WU{i}" for i in range(3)]
                     + [f"WD{i}" for i in range(6)])
        sems = {n: es.enter_context(nc.semaphore(n)) for n in sem_names}
        block = es.enter_context(nc.Block())

        sched = []
        cnt = {n: 0 for n in sem_names}
        _self_sem = {"vector": "SV", "scalar": "SA"}

        def op(engine, fn, waits=(), inc=None):
            best = {}
            ss = _self_sem.get(engine)
            if ss is None and engine == "gpsimd" and inc is not None and inc[0] == "SG":
                ss = "SG"
            waits = list(waits) + ([(ss, cnt[ss])] if ss else [])
            for w in waits:
                if w is None:
                    continue
                s, v = w
                if v > 0 and v > best.get(s, 0):
                    best[s] = v
            incs = []
            if inc is not None:
                cnt[inc[0]] += inc[1]
                incs.append(inc)
            sched.append((engine, fn, list(best.items()), incs))

        def mark(sem):
            return (sem, cnt[sem])

        wg_war = [None] * 3
        wu_war = [None] * 3
        wd_war = [None] * 6
        wg_rdy = [None] * 3
        wu_rdy = [None] * 3
        wd_rdy = [None] * 6
        xt_war = [[], []]
        xt_rdy = [None, None]
        pg_war = [None, None]
        pu_war = [None, None]
        pd_war = [None, None]
        pt_war = [None, None]
        sg_war = [None, None]
        stage_war = [[], []]
        xfm_war = [[], []]
        xsl_war = [[], []]
        xg_war = [[], []]
        osl_war = [[], []]
        comb_war = [[], []]
        acc_war = [[], []]
        h_war = [None]
        o_war = [None, None]
        esb_war = [None]
        big_attn_war = [[]]
        state = {"gi": 0, "di": 0, "pti": 0, "xslslot": 0,
                 "ds_all": None, "sd_all": None, "dg_all": None, "xs_all": None}
        store_marks = [None] * NC_TOK

        op("sync", lambda e: e.dma_start(out=idn[:], in_=idn_in), inc=("SI", 16))
        op("sync", lambda e: e.dma_start(out=ltb[:], in_=lt_in), inc=("SI", 16))
        op("sync", lambda e: e.dma_start(out=ecc[:], in_=ecc_in), inc=("SI", 16))
        op("gpsimd", lambda g: g.dma_start(out=rwT_sb[:], in_=rwTt), inc=("GI", 16))
        op("vector", lambda v: v.memset(eps_t[:], EPS), inc=("SV", 1))
        op("vector", lambda v: v.memset(ones32[:], 1.0), inc=("SV", 1))
        idn_rdy = mark("SI")
        rw_rdy = mark("GI")
        eps_rdy = ("SV", 1)
        ones_rdy = ("SV", 2)

        def load_wg(e_, m, seq):
            slot = seq % 3
            op("gpsimd", (lambda sl, ee, mm: lambda s: s.dma_start(
                out=wg_r[:, sl * 2048:(sl + 1) * 2048], in_=wg_t[ee, mm]))(slot, e_, m),
               waits=[wg_war[slot]], inc=(f"WG{slot}", 16))
            wg_rdy[slot] = mark(f"WG{slot}")

        def load_wu(e_, m, seq):
            slot = seq % 3
            op("gpsimd", (lambda sl, ee, mm: lambda s: s.dma_start(
                out=wu_r[:, sl * 2048:(sl + 1) * 2048], in_=wu_t[ee, mm]))(slot, e_, m),
               waits=[wu_war[slot]], inc=(f"WU{slot}", 16))
            wu_rdy[slot] = mark(f"WU{slot}")

        def load_wd(e_, q, seq):
            slot = seq % 6
            op("gpsimd", (lambda sl, ee, qq: lambda s: s.dma_start(
                out=wd_r[:, sl * 3072:(sl + 1) * 3072], in_=wd_t[ee, qq]))(slot, e_, q),
               waits=[wd_war[slot]], inc=(f"WD{slot}", 16))
            wd_rdy[slot] = mark(f"WD{slot}")

        def ln_chunk(src, tc):
            slot = tc % 2
            stg_v = stage[:, slot * H:(slot + 1) * H]
            op("sync", (lambda sl, r: lambda e: e.dma_start(
                out=x_tm[:, sl * H:(sl + 1) * H], in_=src[r:r + P]))(slot, tc * P),
               waits=xt_war[slot] + [store_marks[tc]], inc=(f"XT{slot}", 16))
            xt_rdy[slot] = mark(f"XT{slot}")
            op("vector", (lambda sl, t: lambda v: v.reduce_sum(
                st_s[:, t:t + 1], x_tm[:, sl * H:(sl + 1) * H], axis=AX.X))(slot, tc),
               waits=[xt_rdy[slot]], inc=("SV", 1))
            s_rdy = mark("SV")
            op("scalar", (lambda sl, t, sv: lambda s: s.activation(
                sv, x_tm[:, sl * H:(sl + 1) * H], AF.Square,
                accum_out=st_q[:, t:t + 1]))(slot, tc, stg_v),
               waits=[xt_rdy[slot]] + stage_war[slot], inc=("SA", 1))
            q_rdy = mark("SA")
            op("vector", (lambda t: lambda v: v.tensor_scalar_mul(
                st_mean[:, t:t + 1], st_s[:, t:t + 1], INV_H))(tc),
               waits=[s_rdy], inc=("SV", 1))
            op("vector", (lambda t: lambda v: v.tensor_mul(
                st_m2[:, t:t + 1], st_mean[:, t:t + 1], st_mean[:, t:t + 1]))(tc),
               waits=[mark("SV")], inc=("SV", 1))
            op("vector", (lambda t: lambda v: v.scalar_tensor_tensor(
                out=st_var[:, t:t + 1], in0=st_q[:, t:t + 1], scalar=INV_H,
                in1=st_m2[:, t:t + 1], op0=ALU.mult, op1=ALU.subtract))(tc),
               waits=[q_rdy, mark("SV")], inc=("SV", 1))
            var_rdy = mark("SV")
            op("scalar", (lambda t: lambda s: s.activation(
                st_ln[:, t:t + 1], st_var[:, t:t + 1], AF.Ln, bias=eps_t[:]))(tc),
               waits=[var_rdy, eps_rdy], inc=("SA", 1))
            op("scalar", (lambda t: lambda s: s.activation(
                st_rstd[:, t:t + 1], st_ln[:, t:t + 1], AF.Exp, scale=-0.5))(tc),
               waits=[mark("SA")], inc=("SA", 1))
            rstd_rdy = mark("SA")
            op("vector", (lambda t: lambda v: v.scalar_tensor_tensor(
                out=st_negmr[:, t:t + 1], in0=st_mean[:, t:t + 1], scalar=-1.0,
                in1=st_rstd[:, t:t + 1], op0=ALU.mult, op1=ALU.mult))(tc),
               waits=[rstd_rdy, mark("SV")], inc=("SV", 1))
            nm_rdy = mark("SV")
            op("scalar", (lambda sl, t, sv: lambda s: s.activation(
                sv, x_tm[:, sl * H:(sl + 1) * H], AF.Identity,
                bias=st_negmr[:, t:t + 1], scale=st_rstd[:, t:t + 1]))(slot, tc, stg_v),
               waits=[nm_rdy, mark("SA")], inc=("SA", 1))
            norm_rdy = mark("SA")
            xt_war[slot] = [norm_rdy, s_rdy]
            return norm_rdy

        def fm_chunk(tc, norm_rdy):
            slot = tc % 2
            evs = []
            for g_ in range(2):
                pti = state["pti"]
                for j in range(8):
                    c = 8 * g_ + j
                    waits = []
                    if j == 0:
                        waits = [norm_rdy, pt_war[pti], idn_rdy]
                    inc = ("SP", 1) if j == 7 else None
                    op("tensor", (lambda pi, sl, cc, jj: lambda t: t.transpose(
                        pt[pi][:, jj * 128:(jj + 1) * 128],
                        stage[:, sl * H + cc * 128: sl * H + (cc + 1) * 128],
                        idn[:]))(pti, slot, c, j),
                       waits=waits, inc=inc)
                tr_done = mark("SP")
                op("vector", (lambda pi, sl, g2: lambda v: v.tensor_copy(
                    x_fm_r[:, sl * H + g2 * 1024: sl * H + (g2 + 1) * 1024],
                    pt[pi][:, 0:1024]))(pti, slot, g_),
                   waits=[tr_done] + xfm_war[slot], inc=("SV", 1))
                pt_war[pti] = mark("SV")
                evs.append(mark("SV"))
                state["pti"] ^= 1
            stage_war[slot] = [tr_done]
            xfm_war[slot] = []
            return evs[-1], tr_done

        def route_chunk(tc, fm_rdy, sub_base):
            slot = tc % 2
            for c in range(HC):
                waits = []
                if c == 0:
                    waits = [fm_rdy, rw_rdy, pg_war[0]]
                inc = ("SP", 1) if c == HC - 1 else None
                op("tensor", (lambda sl, cc: lambda te: te.matmul(
                    pg[0][:, 0:E],
                    x_fm_r[:, sl * H + cc * 128: sl * H + (cc + 1) * 128],
                    rwT_sb[:, cc * E:(cc + 1) * E],
                    start=(cc == 0), stop=(cc == HC - 1)))(slot, c),
                   waits=waits, inc=inc)
            lg_rdy = mark("SP")
            xfm_war[slot].append(lg_rdy)
            op("vector", lambda v: v.tensor_copy(lg[:], pg[0][:, 0:E]),
               waits=[lg_rdy], inc=("SV", 1))
            pg_war[0] = mark("SV")
            op("vector", (lambda t: lambda v: v.reduce_max(
                st_nmax[:, t:t + 1], lg[:], axis=AX.X, negate=True))(tc),
               waits=[mark("SV"), mark("SA")], inc=("SV", 1))
            op("scalar", (lambda t: lambda s: s.activation(
                esb[:], lg[:], AF.Exp, bias=st_nmax[:, t:t + 1]))(tc),
               waits=[mark("SV"), esb_war[0]], inc=("SA", 1))
            e_rdy = mark("SA")
            op("vector", lambda v: v.max(out=m8[:], in_=esb[:]),
               waits=[e_rdy, mark("SV")], inc=("SV", 1))
            op("vector", lambda v: v.match_replace(
                out=zap[:], in_to_replace=m8[:], in_values=esb[:], imm_value=0.0),
               waits=[mark("SV")], inc=("SV", 1))
            op("vector", lambda v: v.tensor_sub(wsel[:], esb[:], zap[:]),
               waits=[mark("SV")], inc=("SV", 1))
            esb_war[0] = mark("SV")
            op("vector", (lambda t: lambda v: v.reduce_sum(
                st_rs[:, t:t + 1], wsel[:], axis=AX.X))(tc),
               waits=[mark("SV")], inc=("SV", 1))
            op("vector", (lambda t: lambda v: v.reciprocal(
                st_rr[:, t:t + 1], st_rs[:, t:t + 1]))(tc),
               waits=[mark("SV")], inc=("SV", 1))
            op("vector", (lambda t: lambda v: v.tensor_scalar_mul(
                w_sb[:], wsel[:], st_rr[:, t:t + 1]))(tc),
               waits=[mark("SV")], inc=("SV", 1))
            op("vector", lambda v: v.tensor_scalar(
                out=mask_f[:], in0=w_sb[:], scalar1=0.0, scalar2=None, op0=ALU.is_gt),
               waits=[mark("SV")], inc=("SV", 1))
            mf_rdy = mark("SV")
            op("vector", (lambda t: lambda v: v.tensor_copy(
                mask_b[:, t * E:(t + 1) * E], mask_f[:]))(tc),
               waits=[mf_rdy], inc=("SV", 1))
            mb_rdy = mark("SV")
            op("vector", lambda v: v.tensor_tensor_scan(
                out=incl32[:], data0=ones32[:], data1=mask_f[:],
                initial=0.0, op0=ALU.mult, op1=ALU.add),
               waits=[mf_rdy, ones_rdy], inc=("SV", 1))
            op("vector", lambda v: v.tensor_sub(excl[:], incl32[:], mask_f[:]),
               waits=[mark("SV")], inc=("SV", 1))
            excl_rdy = mark("SV")
            for i in range(tc + 1):
                waits = []
                if i == 0:
                    waits = [mb_rdy, pg_war[1], idn_rdy]
                inc = ("SP", 1) if i == tc else None
                lsl = (0 if i == tc else P)
                op("tensor", (lambda ii, ls: lambda te: te.matmul(
                    pg[1][:, 0:E],
                    ltb[:, ls:ls + P],
                    mask_b[:, ii * E:(ii + 1) * E],
                    start=(ii == 0), stop=(ii == tc)))(i, lsl),
                   waits=waits, inc=inc)
            pos_rdy = mark("SP")
            op("vector", lambda v: v.tensor_copy(pos_sb[:], pg[1][:, 0:E]),
               waits=[pos_rdy], inc=("SV", 1))
            pg_war[1] = mark("SV")
            op("vector", lambda v: v.tensor_add(slotid[:], pos_sb[:], ecc[:]),
               waits=[mark("SV")], inc=("SV", 1))
            sid_rdy = mark("SV")
            for k in range(K):
                op("vector", (lambda kk: lambda v: v.tensor_scalar(
                    out=eqk[:], in0=excl[:], scalar1=float(kk),
                    scalar2=None, op0=ALU.is_equal))(k),
                   waits=[excl_rdy, mark("SV")], inc=("SV", 1))
                op("vector", lambda v: v.tensor_mul(selk[:], eqk[:], mask_f[:]),
                   waits=[mark("SV")], inc=("SV", 1))
                op("vector", lambda v: v.tensor_mul(tmpk[:], selk[:], slotid[:]),
                   waits=[sid_rdy, mark("SV")], inc=("SV", 1))
                op("vector", lambda v: v.reduce_sum(sk_f[:], tmpk[:], axis=AX.X),
                   waits=[mark("SV")], inc=("SV", 1))
                op("vector", (lambda t, kk: lambda v: v.tensor_copy(
                    sk_u[:, t * K + kk: t * K + kk + 1], sk_f[:]))(tc, k),
                   waits=[mark("SV")], inc=("SV", 1))
                sku_rdy = mark("SV")
                op("vector", lambda v: v.tensor_mul(tmpk[:], selk[:], w_sb[:]),
                   waits=[mark("SV")], inc=("SV", 1))
                op("vector", (lambda t, kk: lambda v: v.reduce_sum(
                    g_k[:, t * K + kk: t * K + kk + 1], tmpk[:], axis=AX.X))(tc, k),
                   waits=[mark("SV")], inc=("SV", 1))
                op("gpsimd", (lambda sl, t, kk, sbv: lambda g: g.indirect_dma_start(
                    out=xslots,
                    out_offset=IndirectOffsetOnAxis(
                        sk_u[:, t * K + kk: t * K + kk + 1], 0),
                    in_=sbv, in_offset=None))(
                        slot, tc, k, stage[:, slot * H:(slot + 1) * H]),
                   waits=[sku_rdy, sub_base], inc=("DS", 16))
                stage_war[slot] = stage_war[slot] + [mark("DS")]

        def emit_tile(e_, st, rdy, ds_all):
            xb = e_ % 2
            xslslot = state["xslslot"]
            state["xslslot"] ^= 1
            op("sync", (lambda ee, ss, xsl_s: lambda e: e.dma_start(
                out=xsl[:, xsl_s * H:(xsl_s + 1) * H],
                in_=xslots[ee * C + ss * P: ee * C + ss * P + P]))(e_, st, xslslot),
               waits=xsl_war[xslslot] + [ds_all], inc=("XS", 16))
            xsl_rdy = mark("XS")
            tr_done = None
            for g_ in range(2):
                pti = state["pti"]
                for j in range(8):
                    c = 8 * g_ + j
                    waits = []
                    if j == 0:
                        waits = [xsl_rdy, pt_war[pti], idn_rdy]
                    inc = ("SP", 1) if j == 7 else None
                    op("tensor", (lambda pi, xs, cc, jj: lambda t: t.transpose(
                        pt[pi][:, jj * 128:(jj + 1) * 128],
                        xsl[:, xs * H + cc * 128: xs * H + (cc + 1) * 128],
                        idn[:]))(pti, xslslot, c, j),
                       waits=waits, inc=inc)
                tr_done = mark("SP")
                op("vector", (lambda pi, xv, g2, ss: lambda v: v.tensor_copy(
                    xv.rearrange("p (c s) -> p c s", c=HC)[
                        :, g2 * 8:(g2 + 1) * 8, ss * P:(ss + 1) * P],
                    pt[pi][:, 0:1024].rearrange("p (b t) -> p b t", b=8)))(
                        pti, xg[xb], g_, st),
                   waits=[tr_done] + xg_war[xb], inc=("SV", 1))
                pt_war[pti] = mark("SV")
                state["pti"] ^= 1
            xg_war[xb] = []
            xsl_war[xslslot] = [tr_done]
            rdy.append(mark("SV"))

        def experts(sweep):
            ds_all = mark("DS")
            load_wg(0, 0, sweep * 7919 + 0)
            load_wu(0, 0, sweep * 7919 + 0)
            load_wg(0, 1, sweep * 7919 + 1)
            load_wu(0, 1, sweep * 7919 + 1)
            load_wd(0, 0, sweep * 7919 + 0)
            tiles_rdy = []
            for st in range(ST):
                emit_tile(0, st, tiles_rdy, ds_all)
            for e_ in range(E):
                xb = e_ % 2
                tile_rdy = tiles_rdy
                h_last = None
                for m in range(IC):
                    seq = sweep * 7919 + e_ * IC + m
                    slot = seq % 3
                    wgr = wg_rdy[slot]
                    wur = wu_rdy[slot]
                    g_done = None
                    u_done = None
                    for n in range(NSUB):
                        xg_rdy = tile_rdy[3 * n + 2]
                        gi = state["gi"]
                        g_b, u_b = pg[gi], pu[gi]
                        for c in range(HC):
                            waits = []
                            if c == 0:
                                waits = [wgr, xg_rdy, pg_war[gi]]
                            inc = ("SP", 1) if c == HC - 1 else None
                            op("tensor", (lambda gb, sl, cc, nn, xv: lambda t: t.matmul(
                                gb[:, 0:TSUB],
                                wg_r[:, sl * 2048 + cc * 128: sl * 2048 + cc * 128 + 128],
                                xv[:, cc * C + nn * TSUB: cc * C + (nn + 1) * TSUB],
                                start=(cc == 0), stop=(cc == HC - 1)))(g_b, slot, c, n, xg[xb]),
                               waits=waits, inc=inc)
                        g_done = mark("SP")
                        for c in range(HC):
                            waits = []
                            if c == 0:
                                waits = [wur, xg_rdy, pu_war[gi]]
                            inc = ("SP", 1) if c == HC - 1 else None
                            op("tensor", (lambda ub, sl, cc, nn, xv: lambda t: t.matmul(
                                ub[:, 0:TSUB],
                                wu_r[:, sl * 2048 + cc * 128: sl * 2048 + cc * 128 + 128],
                                xv[:, cc * C + nn * TSUB: cc * C + (nn + 1) * TSUB],
                                start=(cc == 0), stop=(cc == HC - 1)))(u_b, slot, c, n, xg[xb]),
                               waits=waits, inc=inc)
                        u_done = mark("SP")
                        sgv = sg[:, gi * 512: gi * 512 + TSUB]
                        op("scalar", (lambda gb, sv: lambda s: s.activation(
                            sv, gb[:, 0:TSUB], AF.Silu))(g_b, sgv),
                           waits=[g_done, sg_war[gi]], inc=("SA", 1))
                        pg_war[gi] = mark("SA")
                        silu_done = mark("SA")
                        hv = h_sl[:, m * C + n * TSUB: m * C + (n + 1) * TSUB]
                        op("vector", (lambda ub, sv, hh: lambda v: v.tensor_mul(
                            hh, sv, ub[:, 0:TSUB]))(u_b, sgv, hv),
                           waits=[u_done, silu_done, h_war[0]], inc=("SV", 1))
                        pu_war[gi] = mark("SV")
                        sg_war[gi] = mark("SV")
                        h_last = mark("SV")
                        state["gi"] ^= 1
                    wg_war[slot] = g_done
                    wu_war[slot] = u_done
                    nm, ne_ = m + 2, e_
                    if nm >= IC:
                        nm -= IC
                        ne_ += 1
                    if ne_ < E:
                        load_wg(ne_, nm, sweep * 7919 + ne_ * IC + nm)
                        load_wu(ne_, nm, sweep * 7919 + ne_ * IC + nm)
                xg_war[xb] = [mark("SP")]
                big_attn_war[0] = big_attn_war[0] + [mark("SP")]
                tiles_rdy = []
                for st in range(ST):
                    if e_ + 1 < E:
                        emit_tile(e_ + 1, st, tiles_rdy, ds_all)
                    osl = st % 2
                    q_done = []
                    for q in range(NQ):
                        seq = sweep * 7919 + e_ * NQ + q
                        slot = seq % 6
                        wdr = wd_rdy[slot]
                        if st == 0:
                            nq_, ne_ = q + 1, e_
                            if nq_ >= NQ:
                                nq_ -= NQ
                                ne_ += 1
                            if ne_ < E:
                                load_wd(ne_, nq_, sweep * 7919 + ne_ * NQ + nq_)
                        di = state["di"]
                        d_b = pd[di]
                        for c in range(IC):
                            waits = []
                            if c == 0:
                                waits = [wdr, h_last, pd_war[di]]
                            inc = ("SP", 1) if c == IC - 1 else None
                            op("tensor", (lambda db, sl, cc, qq, ss: lambda t: t.matmul(
                                db[:],
                                h_sl[:, cc * C + ss * P: cc * C + (ss + 1) * P],
                                wd_r[:, sl * 3072 + cc * 512: sl * 3072 + (cc + 1) * 512],
                                start=(cc == 0), stop=(cc == IC - 1)))(d_b, slot, c, q, st),
                               waits=waits, inc=inc)
                        d_done = mark("SP")
                        if st == ST - 1:
                            wd_war[slot] = d_done
                        op("vector", (lambda db, ov, qq: lambda v: v.tensor_copy(
                            ov[:, qq * 512:(qq + 1) * 512], db[:]))(d_b, o_sl[osl], q),
                           waits=[d_done] + osl_war[osl], inc=("SV", 1))
                        pd_war[di] = mark("SV")
                        q_done.append(mark("SV"))
                        state["di"] ^= 1
                    osl_war[osl] = []
                    op("sync", (lambda ov, ee, ss: lambda e: e.dma_start(
                        out=dall[ee * C + ss * P: ee * C + ss * P + P], in_=ov))(
                            o_sl[osl], e_, st),
                       waits=[q_done[-1], state["dg_all"]], inc=("SD", 16))
                    osl_war[osl] = [mark("SD")]
                h_war[0] = mark("SP")
                big_attn_war[0] = big_attn_war[0] + [h_war[0]]
            state["sd_all"] = mark("SD")
            state["xs_all"] = mark("XS")

        def combine(src, dst, scale, chunk_hook=None):
            sd_all = state["sd_all"]
            for tc in range(NC_TOK):
                slot = tc % 2
                av = o_t[:, slot * H:(slot + 1) * H]
                op("sync", (lambda sl, r: lambda e: e.dma_start(
                    out=x_tm[:, sl * H:(sl + 1) * H], in_=src[r:r + P]))(slot, tc * P),
                   waits=xt_war[slot] + [store_marks[tc]], inc=(f"XT{slot}", 16))
                xt_rdy[slot] = mark(f"XT{slot}")
                last_acc = None
                for k in range(K):
                    cb = k % 2
                    op("gpsimd", (lambda t, kk, cv: lambda g: g.indirect_dma_start(
                        out=cv, out_offset=None,
                        in_=dall,
                        in_offset=IndirectOffsetOnAxis(
                            sk_u[:, t * K + kk: t * K + kk + 1], 0)))(tc, k, comb[cb]),
                       waits=[sd_all] + comb_war[cb], inc=("DG", 16))
                    g_rdy = mark("DG")
                    gv = g_k[:, tc * K + k: tc * K + k + 1]
                    if k == 0:
                        op("vector", (lambda cv, aa, gg: lambda v: v.tensor_scalar(
                            out=aa, in0=cv, scalar1=gg, scalar2=None,
                            op0=ALU.mult))(comb[cb], av, gv),
                           waits=[g_rdy, o_war[slot]], inc=("SV", 1))
                    else:
                        op("vector", (lambda cv, aa, gg: lambda v: v.scalar_tensor_tensor(
                            out=aa, in0=cv, scalar=gg, in1=aa,
                            op0=ALU.mult, op1=ALU.add))(comb[cb], av, gv),
                           waits=[g_rdy], inc=("SV", 1))
                    comb_war[cb] = [mark("SV")]
                    last_acc = mark("SV")
                op("vector", (lambda sl, aa, sc: lambda v: v.scalar_tensor_tensor(
                    out=aa, in0=aa, scalar=sc,
                    in1=x_tm[:, sl * H:(sl + 1) * H], op0=ALU.mult, op1=ALU.add))(
                        slot, av, scale),
                   waits=[xt_rdy[slot], last_acc], inc=("SV", 1))
                xt_war[slot] = [mark("SV")]
                stt_done = mark("SV")
                op("sync", (lambda sl, r, aa: lambda e: e.dma_start(
                    out=dst[r:r + P], in_=aa))(slot, tc * P, av),
                   waits=[stt_done], inc=("SO", 16))
                o_war[slot] = mark("SO")
                store_marks[tc] = mark("SO")
                if chunk_hook is not None:
                    chunk_hook(tc)
            state["dg_all"] = mark("DG")

        def ffn_sublayer(src, dst, sweep, scale=0.5, chunk_hook=None):
            sub_base = state["xs_all"]
            for tc in range(NC_TOK):
                norm_rdy = ln_chunk(src, tc)
                fm_rdy, _ = fm_chunk(tc, norm_rdy)
                route_chunk(tc, fm_rdy, sub_base)
            experts(sweep)
            combine(src, dst, scale, chunk_hook=chunk_hook)

        def attn_prep_step(ci):
            slot = ci % 3
            op("gpsimd", (lambda sl, c: lambda s: s.dma_start(
                out=wg_r[:, sl * 2048:(sl + 1) * 2048],
                in_=wTt[:, c * H:(c + 1) * H]))(slot, ci),
               waits=[wg_war[slot]], inc=(f"WG{slot}", 16))
            wg_rdy[slot] = mark(f"WG{slot}")
            op("gpsimd", (lambda sl, c: lambda s: s.dma_start(
                out=wu_r[:, sl * 2048:(sl + 1) * 2048],
                in_=dwTt[:, c * H:(c + 1) * H]))(slot, ci),
               waits=[wu_war[slot]], inc=(f"WU{slot}", 16))
            wu_rdy[slot] = mark(f"WU{slot}")
            op("gpsimd", (lambda sl, c: lambda g: g.tensor_add(
                wsumT[:, c * H:(c + 1) * H],
                wg_r[:, sl * 2048:(sl + 1) * 2048],
                wu_r[:, sl * 2048:(sl + 1) * 2048]))(slot, ci),
               waits=[wg_rdy[slot], wu_rdy[slot], state["sd_all"], h_war[0]]
                     + big_attn_war[0] + xg_war[0] + xg_war[1],
               inc=("SG", 1))
            wg_war[slot] = mark("SG")
            wu_war[slot] = mark("SG")
            if ci == HC - 1:
                big_attn_war[0] = []
                state["wsum_rdy"] = mark("SG")

        def attn_sublayer():
            wsum_rdy = state["wsum_rdy"]
            for tc in range(NC_TOK):
                slot = tc % 2
                norm_rdy = ln_chunk(stream, tc)
                fm_rdy, _ = fm_chunk(tc, norm_rdy)
                for no in range(NQ):
                    di = state["di"]
                    d_b = pd[di]
                    for ci in range(HC):
                        waits = []
                        if ci == 0:
                            waits = [wsum_rdy, fm_rdy, pd_war[di]]
                        inc = ("SP", 1) if ci == HC - 1 else None
                        op("tensor", (lambda db, sl, cc, nn: lambda te: te.matmul(
                            db[:],
                            x_fm_r[:, sl * H + cc * 128: sl * H + (cc + 1) * 128],
                            wsumT[:, cc * H + nn * 512: cc * H + (nn + 1) * 512],
                            start=(cc == 0), stop=(cc == HC - 1)))(d_b, slot, ci, no),
                           waits=waits, inc=inc)
                    d_done = mark("SP")
                    if no == NQ - 1:
                        xfm_war[slot].append(d_done)
                    op("vector", (lambda db, sl, nn: lambda v: v.tensor_add(
                        o_t[:, sl * H + nn * 512: sl * H + (nn + 1) * 512], db[:],
                        x_tm[:, sl * H + nn * 512: sl * H + (nn + 1) * 512]))(d_b, slot, no),
                       waits=[d_done, o_war[slot]], inc=("SV", 1))
                    pd_war[di] = mark("SV")
                    state["di"] ^= 1
                ev_done = mark("SV")
                xt_war[slot] = xt_war[slot] + [ev_done]
                op("sync", (lambda sl, r: lambda e: e.dma_start(
                    out=stream[r:r + P], in_=o_t[:, sl * H:(sl + 1) * H]))(slot, tc * P),
                   waits=[ev_done], inc=("SO", 16))
                o_war[slot] = mark("SO")
                store_marks[tc] = mark("SO")
            m = mark("SP")
            xg_war[0] = xg_war[0] + [m]
            xg_war[1] = xg_war[1] + [m]
            h_war[0] = m

        sweep = 0
        for si, kind in enumerate(SUBS):
            src = x_in if si == 0 else stream
            dst = y_out if si == len(SUBS) - 1 else stream
            if kind == "f":
                nxt_attn = si + 1 < len(SUBS) and SUBS[si + 1] == "a"
                ffn_sublayer(src, dst, sweep,
                             chunk_hook=attn_prep_step if nxt_attn else None)
                sweep += 1
            else:
                attn_sublayer()

        total_so = cnt["SO"]
        op("sync", lambda e: None, waits=[("SO", total_so)])

        def emit(engine_name):
            def run(eng):
                waited = {}
                for (en, fn, ws, incs) in sched:
                    if en != engine_name:
                        continue
                    for (s, v) in ws:
                        if v > waited.get(s, 0):
                            eng.wait_ge(sems[s], v)
                            waited[s] = v
                    ins = fn(eng)
                    for (s, n) in incs:
                        ins.then_inc(sems[s], n)
            return run

        block.vector(emit("vector"))
        block.scalar(emit("scalar"))
        block.tensor(emit("tensor"))
        block.gpsimd(emit("gpsimd"))
        block.sync(emit("sync"))
    return nc


def _tile_inputs(L0_attn_W, L0_attn_delta_W, L0_ffn_router_w, L0_ffn_w_gate,
                 L0_ffn_w_up, L0_ffn_w_down):
    f32 = np.float32
    wg = np.asarray(L0_ffn_w_gate, dtype=f32)
    wu = np.asarray(L0_ffn_w_up, dtype=f32)
    wd = np.asarray(L0_ffn_w_down, dtype=f32)
    wg_t = np.ascontiguousarray(
        wg.reshape(E, HC, P, IC, 128).transpose(0, 3, 2, 1, 4).reshape(E, IC, P, HC * 128))
    wu_t = np.ascontiguousarray(
        wu.reshape(E, HC, P, IC, 128).transpose(0, 3, 2, 1, 4).reshape(E, IC, P, HC * 128))
    wd_t = np.ascontiguousarray(
        wd.reshape(E, IC, P, NQ, 512).transpose(0, 3, 2, 1, 4).reshape(E, NQ, P, IC * 512))
    WT = np.ascontiguousarray(np.asarray(L0_attn_W, dtype=f32).T)
    dWT = np.ascontiguousarray(np.asarray(L0_attn_delta_W, dtype=f32).T)
    wTt = np.ascontiguousarray(WT.reshape(HC, P, H).transpose(1, 0, 2).reshape(P, HC * H))
    dwTt = np.ascontiguousarray(dWT.reshape(HC, P, H).transpose(1, 0, 2).reshape(P, HC * H))
    rwT = np.ascontiguousarray(np.asarray(L0_ffn_router_w, dtype=f32).T)
    rwTt = np.ascontiguousarray(rwT.reshape(HC, P, E).transpose(1, 0, 2).reshape(P, HC * E))
    idn = np.eye(P, dtype=np.float32).astype(ml_dtypes.bfloat16)
    lt = np.triu(np.ones((P, P), dtype=np.float32), 1)
    lt_in = np.concatenate([lt, np.ones((P, P), dtype=np.float32)],
                           axis=1).astype(ml_dtypes.bfloat16)
    ecc_in = np.broadcast_to(
        (np.arange(E, dtype=np.float32) * C)[None, :], (P, E)).copy()
    return dict(wg_t=wg_t, wu_t=wu_t, wd_t=wd_t, wTt=wTt, dwTt=dwTt,
                rwTt=rwTt, idn_in=idn, lt_in=lt_in, ecc_in=ecc_in)


def kernel(L0_attn_x, L0_attn_W, L0_attn_target_W, L0_attn_delta_W,
           L0_ffn_router_w, L0_ffn_w_gate, L0_ffn_w_up, L0_ffn_w_down):
    assert np.asarray(L0_attn_target_W).shape == (H, H)
    x = np.ascontiguousarray(np.asarray(L0_attn_x, dtype=np.float32).reshape(16384, H))
    shared = _tile_inputs(L0_attn_W, L0_attn_delta_W, L0_ffn_router_w,
                          L0_ffn_w_gate, L0_ffn_w_up, L0_ffn_w_down)
    nc = build_nc()
    in_maps = [dict(shared, x_in=np.ascontiguousarray(x[c * NT:(c + 1) * NT]))
               for c in range(NCORES)]
    res = run_bass_kernel_spmd(nc, in_maps, core_ids=list(range(NCORES)))
    y = np.concatenate([np.asarray(res.results[c]["y"]) for c in range(NCORES)], axis=0)
    return y.reshape(4, 4096, H).astype(np.float32)


# version 07
# speedup vs baseline: 1.1035x; 1.1014x over previous
import numpy as np
import ml_dtypes
from contextlib import ExitStack
import concourse.bass as bass
import concourse.mybir as mybir
from concourse.bass_utils import run_bass_kernel_spmd
from concourse.bass_core_rust import IndirectOffsetOnAxis

F32 = mybir.dt.float32
BF16 = mybir.dt.bfloat16
U32 = mybir.dt.uint32
AF = mybir.ActivationFunctionType
ALU = mybir.AluOpType
AX = mybir.AxisListType

P = 128
NCORES = 8
H = 2048
HC = H // P
I = 768
IC = I // P
NQ = 4
E = 32
K = 8
C = 768
ST = C // P
NSUB = 2
TSUB = C // NSUB
S_TOT = E * C
NC_TOK = 16
NT = NC_TOK * P
EPS = 1e-5
INV_H = 1.0 / H

SUBS = ("f", "a", "f")


def build_nc():
    nc = bass.Bass("TRN2", target_bir_lowering=False)
    dram = nc.dram_tensor
    x_in = dram("x_in", [NT, H], F32, kind="ExternalInput").ap()
    wg_t = dram("wg_t", [E, IC, P, HC * 128], F32, kind="ExternalInput").ap()
    wu_t = dram("wu_t", [E, IC, P, HC * 128], F32, kind="ExternalInput").ap()
    wd_t = dram("wd_t", [E, NQ, P, IC * 512], F32, kind="ExternalInput").ap()
    wTt = dram("wTt", [P, HC * H], F32, kind="ExternalInput").ap()
    dwTt = dram("dwTt", [P, HC * H], F32, kind="ExternalInput").ap()
    rwTt = dram("rwTt", [P, HC * E], F32, kind="ExternalInput").ap()
    idn_in = dram("idn_in", [P, P], BF16, kind="ExternalInput").ap()
    lt_in = dram("lt_in", [P, 2 * P], BF16, kind="ExternalInput").ap()
    ecc_in = dram("ecc_in", [P, E], F32, kind="ExternalInput").ap()
    y_out = dram("y", [NT, H], F32, kind="ExternalOutput").ap()
    stream = dram("stream", [NT, H], F32, kind="Internal").ap()
    xslots = dram("xslots", [S_TOT, H], BF16, kind="Internal").ap()
    dall = dram("dall", [S_TOT, H], BF16, kind="Internal").ap()

    es = ExitStack()
    with es:
        sb = lambda name, shape, dt: es.enter_context(nc.sbuf_tensor(name, shape, dt))
        pst = lambda name, shape, dt: es.enter_context(nc.psum_tensor(name, shape, dt))
        big = sb("big", [P, 38912], BF16)
        xg = [big[:, 0:12288], big[:, 12288:24576]]
        h_sl = big[:, 24576:29184]
        o_sl = [big[:, 29184:31232], big[:, 31232:33280]]
        comb = [big[:, 33280:35328], big[:, 35328:37376]]
        wsumT = big[:, 0:32768]

        wg_r = sb("wg_r", [P, 3 * 2048], BF16)
        wu_r = sb("wu_r", [P, 3 * 2048], BF16)
        wd_r = sb("wd_r", [P, 6 * 3072], BF16)
        xsl = sb("xsl", [P, 2 * H], BF16)
        x_tm = sb("x_tm", [P, 2 * H], F32)
        stage = sb("stage", [P, 2 * H], BF16)
        x_fm_r = sb("x_fm_r", [P, 2 * H], BF16)
        o_t = sb("o_t", [P, 2 * H], F32)
        sg = sb("sg", [P, 2 * 512], F32)
        lg = sb("lg", [P, E], F32)
        w_sb = sb("w_sb", [P, E], F32)
        mask_f = sb("mask_f", [P, E], F32)
        mask_b = sb("mask_b", [P, NC_TOK * E], BF16)
        pos_sb = sb("pos_sb", [P, E], F32)
        excl = sb("excl", [P, E], F32)
        slotid = sb("slotid", [P, E], F32)
        eqk = sb("eqk", [P, E], F32)
        selk = sb("selk", [P, E], F32)
        tmpk = sb("tmpk", [P, E], F32)
        ones32 = sb("ones32", [P, E], F32)
        incl32 = sb("incl32", [P, E], F32)
        sk_f = sb("sk_f", [P, 1], F32)
        sk_u = sb("sk_u", [P, NC_TOK * K], U32)
        g_k = sb("g_k", [P, NC_TOK * K], F32)
        esb = sb("esb", [P, E], F32)
        zap = sb("zap", [P, E], F32)
        wsel = sb("wsel", [P, E], F32)
        m8 = sb("m8", [P, 8], F32)
        rwT_sb = sb("rwT_sb", [P, HC * E], BF16)
        idn = sb("idn", [P, P], BF16)
        ltb = sb("ltb", [P, 2 * P], BF16)
        ecc = sb("ecc", [P, E], F32)
        st_s = sb("st_s", [P, NC_TOK], F32)
        st_q = sb("st_q", [P, NC_TOK], F32)
        st_mean = sb("st_mean", [P, NC_TOK], F32)
        st_m2 = sb("st_m2", [P, NC_TOK], F32)
        st_var = sb("st_var", [P, NC_TOK], F32)
        st_ln = sb("st_ln", [P, NC_TOK], F32)
        st_rstd = sb("st_rstd", [P, NC_TOK], F32)
        st_negmr = sb("st_negmr", [P, NC_TOK], F32)
        st_nmax = sb("st_nmax", [P, NC_TOK], F32)
        st_rs = sb("st_rs", [P, NC_TOK], F32)
        st_rr = sb("st_rr", [P, NC_TOK], F32)
        eps_t = sb("eps_t", [P, 1], F32)

        pg = [pst("pg0", [P, 512], F32), pst("pg1", [P, 512], F32)]
        pu = [pst("pu0", [P, 512], F32), pst("pu1", [P, 512], F32)]
        pd = [pst("pd0", [P, 512], F32), pst("pd1", [P, 512], F32)]
        pt = [pst("pt0", [P, 1024], BF16), pst("pt1", [P, 1024], BF16)]

        sem_names = (["SP", "SV", "SA", "SG", "GI", "SI", "SO", "XT0", "XT1",
                      "DS", "DG", "SD", "XS"]
                     + [f"WG{i}" for i in range(3)] + [f"WU{i}" for i in range(3)]
                     + [f"WD{i}" for i in range(6)])
        sems = {n: es.enter_context(nc.semaphore(n)) for n in sem_names}
        block = es.enter_context(nc.Block())

        sched = []
        cnt = {n: 0 for n in sem_names}
        _self_sem = {"vector": "SV", "scalar": "SA"}

        def op(engine, fn, waits=(), inc=None):
            best = {}
            ss = _self_sem.get(engine)
            if ss is None and engine == "gpsimd" and inc is not None and inc[0] == "SG":
                ss = "SG"
            waits = list(waits) + ([(ss, cnt[ss])] if ss else [])
            for w in waits:
                if w is None:
                    continue
                s, v = w
                if v > 0 and v > best.get(s, 0):
                    best[s] = v
            incs = []
            if inc is not None:
                cnt[inc[0]] += inc[1]
                incs.append(inc)
            sched.append((engine, fn, list(best.items()), incs))

        def mark(sem):
            return (sem, cnt[sem])

        wg_war = [None] * 3
        wu_war = [None] * 3
        wd_war = [None] * 6
        wg_rdy = [None] * 3
        wu_rdy = [None] * 3
        wd_rdy = [None] * 6
        xt_war = [[], []]
        xt_rdy = [None, None]
        pg_war = [None, None]
        pu_war = [None, None]
        pd_war = [None, None]
        pt_war = [None, None]
        sg_war = [None, None]
        stage_war = [[], []]
        xfm_war = [[], []]
        xsl_war = [[], []]
        xg_war = [[], []]
        osl_war = [[], []]
        comb_war = [[], []]
        acc_war = [[], []]
        h_war = [None]
        o_war = [None, None]
        esb_war = [None]
        big_attn_war = [[]]
        state = {"gi": 0, "di": 0, "pti": 0, "xslslot": 0,
                 "ds_all": None, "sd_all": None, "dg_all": None, "xs_all": None}
        store_marks = [None] * NC_TOK

        op("sync", lambda e: e.dma_start(out=idn[:], in_=idn_in), inc=("SI", 16))
        op("sync", lambda e: e.dma_start(out=ltb[:], in_=lt_in), inc=("SI", 16))
        op("sync", lambda e: e.dma_start(out=ecc[:], in_=ecc_in), inc=("SI", 16))
        op("gpsimd", lambda g: g.dma_start(out=rwT_sb[:], in_=rwTt), inc=("GI", 16))
        op("vector", lambda v: v.memset(eps_t[:], EPS), inc=("SV", 1))
        op("vector", lambda v: v.memset(ones32[:], 1.0), inc=("SV", 1))
        idn_rdy = mark("SI")
        rw_rdy = mark("GI")
        eps_rdy = ("SV", 1)
        ones_rdy = ("SV", 2)

        def load_wg(e_, m, seq):
            slot = seq % 3
            op("gpsimd", (lambda sl, ee, mm: lambda s: s.dma_start(
                out=wg_r[:, sl * 2048:(sl + 1) * 2048], in_=wg_t[ee, mm]))(slot, e_, m),
               waits=[wg_war[slot]], inc=(f"WG{slot}", 16))
            wg_rdy[slot] = mark(f"WG{slot}")

        def load_wu(e_, m, seq):
            slot = seq % 3
            op("gpsimd", (lambda sl, ee, mm: lambda s: s.dma_start(
                out=wu_r[:, sl * 2048:(sl + 1) * 2048], in_=wu_t[ee, mm]))(slot, e_, m),
               waits=[wu_war[slot]], inc=(f"WU{slot}", 16))
            wu_rdy[slot] = mark(f"WU{slot}")

        def load_wd(e_, q, seq):
            slot = seq % 6
            op("gpsimd", (lambda sl, ee, qq: lambda s: s.dma_start(
                out=wd_r[:, sl * 3072:(sl + 1) * 3072], in_=wd_t[ee, qq]))(slot, e_, q),
               waits=[wd_war[slot]], inc=(f"WD{slot}", 16))
            wd_rdy[slot] = mark(f"WD{slot}")

        def ln_chunk(src, tc):
            slot = tc % 2
            stg_v = stage[:, slot * H:(slot + 1) * H]
            op("sync", (lambda sl, r: lambda e: e.dma_start(
                out=x_tm[:, sl * H:(sl + 1) * H], in_=src[r:r + P]))(slot, tc * P),
               waits=xt_war[slot] + [store_marks[tc]], inc=(f"XT{slot}", 16))
            xt_rdy[slot] = mark(f"XT{slot}")
            op("vector", (lambda sl, t: lambda v: v.reduce_sum(
                st_s[:, t:t + 1], x_tm[:, sl * H:(sl + 1) * H], axis=AX.X))(slot, tc),
               waits=[xt_rdy[slot]], inc=("SV", 1))
            s_rdy = mark("SV")
            op("scalar", (lambda sl, t, sv: lambda s: s.activation(
                sv, x_tm[:, sl * H:(sl + 1) * H], AF.Square,
                accum_out=st_q[:, t:t + 1]))(slot, tc, stg_v),
               waits=[xt_rdy[slot]] + stage_war[slot], inc=("SA", 1))
            q_rdy = mark("SA")
            op("vector", (lambda t: lambda v: v.tensor_scalar_mul(
                st_mean[:, t:t + 1], st_s[:, t:t + 1], INV_H))(tc),
               waits=[s_rdy], inc=("SV", 1))
            op("vector", (lambda t: lambda v: v.tensor_mul(
                st_m2[:, t:t + 1], st_mean[:, t:t + 1], st_mean[:, t:t + 1]))(tc),
               waits=[mark("SV")], inc=("SV", 1))
            op("vector", (lambda t: lambda v: v.scalar_tensor_tensor(
                out=st_var[:, t:t + 1], in0=st_q[:, t:t + 1], scalar=INV_H,
                in1=st_m2[:, t:t + 1], op0=ALU.mult, op1=ALU.subtract))(tc),
               waits=[q_rdy, mark("SV")], inc=("SV", 1))
            var_rdy = mark("SV")
            op("scalar", (lambda t: lambda s: s.activation(
                st_ln[:, t:t + 1], st_var[:, t:t + 1], AF.Ln, bias=eps_t[:]))(tc),
               waits=[var_rdy, eps_rdy], inc=("SA", 1))
            op("scalar", (lambda t: lambda s: s.activation(
                st_rstd[:, t:t + 1], st_ln[:, t:t + 1], AF.Exp, scale=-0.5))(tc),
               waits=[mark("SA")], inc=("SA", 1))
            rstd_rdy = mark("SA")
            op("vector", (lambda t: lambda v: v.scalar_tensor_tensor(
                out=st_negmr[:, t:t + 1], in0=st_mean[:, t:t + 1], scalar=-1.0,
                in1=st_rstd[:, t:t + 1], op0=ALU.mult, op1=ALU.mult))(tc),
               waits=[rstd_rdy, mark("SV")], inc=("SV", 1))
            nm_rdy = mark("SV")
            op("scalar", (lambda sl, t, sv: lambda s: s.activation(
                sv, x_tm[:, sl * H:(sl + 1) * H], AF.Identity,
                bias=st_negmr[:, t:t + 1], scale=st_rstd[:, t:t + 1]))(slot, tc, stg_v),
               waits=[nm_rdy, mark("SA")], inc=("SA", 1))
            norm_rdy = mark("SA")
            xt_war[slot] = [norm_rdy, s_rdy]
            return norm_rdy

        def fm_chunk(tc, norm_rdy):
            slot = tc % 2
            evs = []
            for g_ in range(2):
                pti = state["pti"]
                for j in range(8):
                    c = 8 * g_ + j
                    waits = []
                    if j == 0:
                        waits = [norm_rdy, pt_war[pti], idn_rdy]
                    inc = ("SP", 1) if j == 7 else None
                    op("tensor", (lambda pi, sl, cc, jj: lambda t: t.transpose(
                        pt[pi][:, jj * 128:(jj + 1) * 128],
                        stage[:, sl * H + cc * 128: sl * H + (cc + 1) * 128],
                        idn[:]))(pti, slot, c, j),
                       waits=waits, inc=inc)
                tr_done = mark("SP")
                op("vector", (lambda pi, sl, g2: lambda v: v.tensor_copy(
                    x_fm_r[:, sl * H + g2 * 1024: sl * H + (g2 + 1) * 1024],
                    pt[pi][:, 0:1024]))(pti, slot, g_),
                   waits=[tr_done] + xfm_war[slot], inc=("SV", 1))
                pt_war[pti] = mark("SV")
                evs.append(mark("SV"))
                state["pti"] ^= 1
            stage_war[slot] = [tr_done]
            xfm_war[slot] = []
            return evs[-1], tr_done

        def route_chunk(tc, fm_rdy, sub_base):
            slot = tc % 2
            for c in range(HC):
                waits = []
                if c == 0:
                    waits = [fm_rdy, rw_rdy, pg_war[0]]
                inc = ("SP", 1) if c == HC - 1 else None
                op("tensor", (lambda sl, cc: lambda te: te.matmul(
                    pg[0][:, 0:E],
                    x_fm_r[:, sl * H + cc * 128: sl * H + (cc + 1) * 128],
                    rwT_sb[:, cc * E:(cc + 1) * E],
                    start=(cc == 0), stop=(cc == HC - 1)))(slot, c),
                   waits=waits, inc=inc)
            lg_rdy = mark("SP")
            xfm_war[slot].append(lg_rdy)
            op("vector", lambda v: v.tensor_copy(lg[:], pg[0][:, 0:E]),
               waits=[lg_rdy], inc=("SV", 1))
            pg_war[0] = mark("SV")
            op("vector", (lambda t: lambda v: v.reduce_max(
                st_nmax[:, t:t + 1], lg[:], axis=AX.X, negate=True))(tc),
               waits=[mark("SV"), mark("SA")], inc=("SV", 1))
            op("scalar", (lambda t: lambda s: s.activation(
                esb[:], lg[:], AF.Exp, bias=st_nmax[:, t:t + 1]))(tc),
               waits=[mark("SV"), esb_war[0]], inc=("SA", 1))
            e_rdy = mark("SA")
            op("vector", lambda v: v.max(out=m8[:], in_=esb[:]),
               waits=[e_rdy, mark("SV")], inc=("SV", 1))
            op("vector", lambda v: v.match_replace(
                out=zap[:], in_to_replace=m8[:], in_values=esb[:], imm_value=0.0),
               waits=[mark("SV")], inc=("SV", 1))
            op("vector", lambda v: v.tensor_sub(wsel[:], esb[:], zap[:]),
               waits=[mark("SV")], inc=("SV", 1))
            esb_war[0] = mark("SV")
            op("vector", (lambda t: lambda v: v.reduce_sum(
                st_rs[:, t:t + 1], wsel[:], axis=AX.X))(tc),
               waits=[mark("SV")], inc=("SV", 1))
            op("vector", (lambda t: lambda v: v.reciprocal(
                st_rr[:, t:t + 1], st_rs[:, t:t + 1]))(tc),
               waits=[mark("SV")], inc=("SV", 1))
            op("vector", (lambda t: lambda v: v.tensor_scalar_mul(
                w_sb[:], wsel[:], st_rr[:, t:t + 1]))(tc),
               waits=[mark("SV")], inc=("SV", 1))
            op("vector", lambda v: v.tensor_scalar(
                out=mask_f[:], in0=w_sb[:], scalar1=0.0, scalar2=None, op0=ALU.is_gt),
               waits=[mark("SV")], inc=("SV", 1))
            mf_rdy = mark("SV")
            op("vector", (lambda t: lambda v: v.tensor_copy(
                mask_b[:, t * E:(t + 1) * E], mask_f[:]))(tc),
               waits=[mf_rdy], inc=("SV", 1))
            mb_rdy = mark("SV")
            op("vector", lambda v: v.tensor_tensor_scan(
                out=incl32[:], data0=ones32[:], data1=mask_f[:],
                initial=0.0, op0=ALU.mult, op1=ALU.add),
               waits=[mf_rdy, ones_rdy], inc=("SV", 1))
            op("vector", lambda v: v.tensor_sub(excl[:], incl32[:], mask_f[:]),
               waits=[mark("SV")], inc=("SV", 1))
            excl_rdy = mark("SV")
            for i in range(tc + 1):
                waits = []
                if i == 0:
                    waits = [mb_rdy, pg_war[1], idn_rdy]
                inc = ("SP", 1) if i == tc else None
                lsl = (0 if i == tc else P)
                op("tensor", (lambda ii, ls: lambda te: te.matmul(
                    pg[1][:, 0:E],
                    ltb[:, ls:ls + P],
                    mask_b[:, ii * E:(ii + 1) * E],
                    start=(ii == 0), stop=(ii == tc)))(i, lsl),
                   waits=waits, inc=inc)
            pos_rdy = mark("SP")
            op("vector", lambda v: v.tensor_copy(pos_sb[:], pg[1][:, 0:E]),
               waits=[pos_rdy], inc=("SV", 1))
            pg_war[1] = mark("SV")
            op("vector", lambda v: v.tensor_add(slotid[:], pos_sb[:], ecc[:]),
               waits=[mark("SV")], inc=("SV", 1))
            sid_rdy = mark("SV")
            for k in range(K):
                op("vector", (lambda kk: lambda v: v.tensor_scalar(
                    out=eqk[:], in0=excl[:], scalar1=float(kk),
                    scalar2=None, op0=ALU.is_equal))(k),
                   waits=[excl_rdy, mark("SV")], inc=("SV", 1))
                op("vector", lambda v: v.tensor_mul(selk[:], eqk[:], mask_f[:]),
                   waits=[mark("SV")], inc=("SV", 1))
                op("vector", lambda v: v.tensor_mul(tmpk[:], selk[:], slotid[:]),
                   waits=[sid_rdy, mark("SV")], inc=("SV", 1))
                op("vector", lambda v: v.reduce_sum(sk_f[:], tmpk[:], axis=AX.X),
                   waits=[mark("SV")], inc=("SV", 1))
                op("vector", (lambda t, kk: lambda v: v.tensor_copy(
                    sk_u[:, t * K + kk: t * K + kk + 1], sk_f[:]))(tc, k),
                   waits=[mark("SV")], inc=("SV", 1))
                sku_rdy = mark("SV")
                op("vector", lambda v: v.tensor_mul(tmpk[:], selk[:], w_sb[:]),
                   waits=[mark("SV")], inc=("SV", 1))
                op("vector", (lambda t, kk: lambda v: v.reduce_sum(
                    g_k[:, t * K + kk: t * K + kk + 1], tmpk[:], axis=AX.X))(tc, k),
                   waits=[mark("SV")], inc=("SV", 1))
                op("gpsimd", (lambda sl, t, kk, sbv: lambda g: g.indirect_dma_start(
                    out=xslots,
                    out_offset=IndirectOffsetOnAxis(
                        sk_u[:, t * K + kk: t * K + kk + 1], 0),
                    in_=sbv, in_offset=None))(
                        slot, tc, k, stage[:, slot * H:(slot + 1) * H]),
                   waits=[sku_rdy, sub_base], inc=("DS", 16))
                stage_war[slot] = stage_war[slot] + [mark("DS")]

        def tile_load(e_, st, ds_all):
            xslslot = state["xslslot"]
            state["xslslot"] ^= 1
            op("sync", (lambda ee, ss, xsl_s: lambda e: e.dma_start(
                out=xsl[:, xsl_s * H:(xsl_s + 1) * H],
                in_=xslots[ee * C + ss * P: ee * C + ss * P + P]))(e_, st, xslslot),
               waits=xsl_war[xslslot] + [ds_all], inc=("XS", 16))
            return xslslot, mark("XS")

        def emit_tile(e_, st, rdy, loaded):
            xb = e_ % 2
            xslslot, xsl_rdy = loaded
            tr_done = None
            for g_ in range(2):
                pti = state["pti"]
                for j in range(8):
                    c = 8 * g_ + j
                    waits = []
                    if j == 0:
                        waits = [xsl_rdy, pt_war[pti], idn_rdy]
                    inc = ("SP", 1) if j == 7 else None
                    op("tensor", (lambda pi, xs, cc, jj: lambda t: t.transpose(
                        pt[pi][:, jj * 128:(jj + 1) * 128],
                        xsl[:, xs * H + cc * 128: xs * H + (cc + 1) * 128],
                        idn[:]))(pti, xslslot, c, j),
                       waits=waits, inc=inc)
                tr_done = mark("SP")
                op("vector", (lambda pi, xv, g2, ss: lambda v: v.tensor_copy(
                    xv.rearrange("p (c s) -> p c s", c=HC)[
                        :, g2 * 8:(g2 + 1) * 8, ss * P:(ss + 1) * P],
                    pt[pi][:, 0:1024].rearrange("p (b t) -> p b t", b=8)))(
                        pti, xg[xb], g_, st),
                   waits=[tr_done] + xg_war[xb], inc=("SV", 1))
                pt_war[pti] = mark("SV")
                state["pti"] ^= 1
            xg_war[xb] = []
            xsl_war[xslslot] = [tr_done]
            rdy.append(mark("SV"))

        def experts(sweep):
            ds_all = mark("DS")
            load_wg(0, 0, sweep * 7919 + 0)
            load_wu(0, 0, sweep * 7919 + 0)
            load_wg(0, 1, sweep * 7919 + 1)
            load_wu(0, 1, sweep * 7919 + 1)
            load_wd(0, 0, sweep * 7919 + 0)
            tiles_rdy = []
            for st in range(ST):
                emit_tile(0, st, tiles_rdy, tile_load(0, st, ds_all))
            for e_ in range(E):
                xb = e_ % 2
                tile_rdy = tiles_rdy
                h_last = None
                for m in range(IC):
                    seq = sweep * 7919 + e_ * IC + m
                    slot = seq % 3
                    wgr = wg_rdy[slot]
                    wur = wu_rdy[slot]
                    g_done = None
                    u_done = None
                    for n in range(NSUB):
                        xg_rdy = tile_rdy[3 * n + 2]
                        gi = state["gi"]
                        g_b, u_b = pg[gi], pu[gi]
                        for c in range(HC):
                            waits = []
                            if c == 0:
                                waits = [wgr, xg_rdy, pg_war[gi]]
                            inc = ("SP", 1) if c == HC - 1 else None
                            op("tensor", (lambda gb, sl, cc, nn, xv: lambda t: t.matmul(
                                gb[:, 0:TSUB],
                                wg_r[:, sl * 2048 + cc * 128: sl * 2048 + cc * 128 + 128],
                                xv[:, cc * C + nn * TSUB: cc * C + (nn + 1) * TSUB],
                                start=(cc == 0), stop=(cc == HC - 1)))(g_b, slot, c, n, xg[xb]),
                               waits=waits, inc=inc)
                        g_done = mark("SP")
                        for c in range(HC):
                            waits = []
                            if c == 0:
                                waits = [wur, xg_rdy, pu_war[gi]]
                            inc = ("SP", 1) if c == HC - 1 else None
                            op("tensor", (lambda ub, sl, cc, nn, xv: lambda t: t.matmul(
                                ub[:, 0:TSUB],
                                wu_r[:, sl * 2048 + cc * 128: sl * 2048 + cc * 128 + 128],
                                xv[:, cc * C + nn * TSUB: cc * C + (nn + 1) * TSUB],
                                start=(cc == 0), stop=(cc == HC - 1)))(u_b, slot, c, n, xg[xb]),
                               waits=waits, inc=inc)
                        u_done = mark("SP")
                        sgv = sg[:, gi * 512: gi * 512 + TSUB]
                        op("scalar", (lambda gb, sv: lambda s: s.activation(
                            sv, gb[:, 0:TSUB], AF.Silu))(g_b, sgv),
                           waits=[g_done, sg_war[gi]], inc=("SA", 1))
                        pg_war[gi] = mark("SA")
                        silu_done = mark("SA")
                        hv = h_sl[:, m * C + n * TSUB: m * C + (n + 1) * TSUB]
                        op("vector", (lambda ub, sv, hh: lambda v: v.tensor_mul(
                            hh, sv, ub[:, 0:TSUB]))(u_b, sgv, hv),
                           waits=[u_done, silu_done, h_war[0]], inc=("SV", 1))
                        pu_war[gi] = mark("SV")
                        sg_war[gi] = mark("SV")
                        h_last = mark("SV")
                        state["gi"] ^= 1
                    wg_war[slot] = g_done
                    wu_war[slot] = u_done
                    nm, ne_ = m + 2, e_
                    if nm >= IC:
                        nm -= IC
                        ne_ += 1
                    if ne_ < E:
                        load_wg(ne_, nm, sweep * 7919 + ne_ * IC + nm)
                        load_wu(ne_, nm, sweep * 7919 + ne_ * IC + nm)
                xg_war[xb] = [mark("SP")]
                big_attn_war[0] = big_attn_war[0] + [mark("SP")]
                tiles_rdy = []
                if e_ + 1 < E:
                    state["pend_tile"] = (0, tile_load(e_ + 1, 0, ds_all))
                for st in range(ST):
                    osl = st % 2
                    q_done = []
                    for q in range(NQ):
                        seq = sweep * 7919 + e_ * NQ + q
                        slot = seq % 6
                        wdr = wd_rdy[slot]
                        if st == 0:
                            nq_, ne_ = q + 1, e_
                            if nq_ >= NQ:
                                nq_ -= NQ
                                ne_ += 1
                            if ne_ < E:
                                load_wd(ne_, nq_, sweep * 7919 + ne_ * NQ + nq_)
                        di = state["di"]
                        d_b = pd[di]
                        for c in range(IC):
                            waits = []
                            if c == 0:
                                waits = [wdr, h_last, pd_war[di]]
                            inc = ("SP", 1) if c == IC - 1 else None
                            op("tensor", (lambda db, sl, cc, qq, ss: lambda t: t.matmul(
                                db[:],
                                h_sl[:, cc * C + ss * P: cc * C + (ss + 1) * P],
                                wd_r[:, sl * 3072 + cc * 512: sl * 3072 + (cc + 1) * 512],
                                start=(cc == 0), stop=(cc == IC - 1)))(d_b, slot, c, q, st),
                               waits=waits, inc=inc)
                        d_done = mark("SP")
                        if st == ST - 1:
                            wd_war[slot] = d_done
                        op("vector", (lambda db, ov, qq: lambda v: v.tensor_copy(
                            ov[:, qq * 512:(qq + 1) * 512], db[:]))(d_b, o_sl[osl], q),
                           waits=[d_done] + osl_war[osl], inc=("SV", 1))
                        pd_war[di] = mark("SV")
                        q_done.append(mark("SV"))
                        state["di"] ^= 1
                    osl_war[osl] = []
                    op("sync", (lambda ov, ee, ss: lambda e: e.dma_start(
                        out=dall[ee * C + ss * P: ee * C + ss * P + P], in_=ov))(
                            o_sl[osl], e_, st),
                       waits=[q_done[-1], state["dg_all"]], inc=("SD", 16))
                    osl_war[osl] = [mark("SD")]
                    if e_ + 1 < E:
                        pst_, loaded = state["pend_tile"]
                        emit_tile(e_ + 1, pst_, tiles_rdy, loaded)
                        if pst_ + 1 < ST:
                            state["pend_tile"] = (pst_ + 1,
                                                  tile_load(e_ + 1, pst_ + 1, ds_all))
                h_war[0] = mark("SP")
                big_attn_war[0] = big_attn_war[0] + [h_war[0]]
            state["sd_all"] = mark("SD")
            state["xs_all"] = mark("XS")

        def combine(src, dst, scale, chunk_hook=None):
            sd_all = state["sd_all"]
            for tc in range(NC_TOK):
                slot = tc % 2
                av = o_t[:, slot * H:(slot + 1) * H]
                op("sync", (lambda sl, r: lambda e: e.dma_start(
                    out=x_tm[:, sl * H:(sl + 1) * H], in_=src[r:r + P]))(slot, tc * P),
                   waits=xt_war[slot] + [store_marks[tc]], inc=(f"XT{slot}", 16))
                xt_rdy[slot] = mark(f"XT{slot}")
                last_acc = None
                for k in range(K):
                    cb = k % 2
                    op("gpsimd", (lambda t, kk, cv: lambda g: g.indirect_dma_start(
                        out=cv, out_offset=None,
                        in_=dall,
                        in_offset=IndirectOffsetOnAxis(
                            sk_u[:, t * K + kk: t * K + kk + 1], 0)))(tc, k, comb[cb]),
                       waits=[sd_all] + comb_war[cb], inc=("DG", 16))
                    g_rdy = mark("DG")
                    gv = g_k[:, tc * K + k: tc * K + k + 1]
                    if k == 0:
                        op("vector", (lambda cv, aa, gg: lambda v: v.tensor_scalar(
                            out=aa, in0=cv, scalar1=gg, scalar2=None,
                            op0=ALU.mult))(comb[cb], av, gv),
                           waits=[g_rdy, o_war[slot]], inc=("SV", 1))
                    else:
                        op("vector", (lambda cv, aa, gg: lambda v: v.scalar_tensor_tensor(
                            out=aa, in0=cv, scalar=gg, in1=aa,
                            op0=ALU.mult, op1=ALU.add))(comb[cb], av, gv),
                           waits=[g_rdy], inc=("SV", 1))
                    comb_war[cb] = [mark("SV")]
                    last_acc = mark("SV")
                op("vector", (lambda sl, aa, sc: lambda v: v.scalar_tensor_tensor(
                    out=aa, in0=aa, scalar=sc,
                    in1=x_tm[:, sl * H:(sl + 1) * H], op0=ALU.mult, op1=ALU.add))(
                        slot, av, scale),
                   waits=[xt_rdy[slot], last_acc], inc=("SV", 1))
                xt_war[slot] = [mark("SV")]
                stt_done = mark("SV")
                op("sync", (lambda sl, r, aa: lambda e: e.dma_start(
                    out=dst[r:r + P], in_=aa))(slot, tc * P, av),
                   waits=[stt_done], inc=("SO", 16))
                o_war[slot] = mark("SO")
                store_marks[tc] = mark("SO")
                if chunk_hook is not None:
                    chunk_hook(tc)
            state["dg_all"] = mark("DG")

        def ffn_sublayer(src, dst, sweep, scale=0.5, chunk_hook=None):
            sub_base = state["xs_all"]
            for tc in range(NC_TOK):
                norm_rdy = ln_chunk(src, tc)
                fm_rdy, _ = fm_chunk(tc, norm_rdy)
                route_chunk(tc, fm_rdy, sub_base)
            experts(sweep)
            combine(src, dst, scale, chunk_hook=chunk_hook)

        def attn_prep_step(ci):
            slot = ci % 3
            op("gpsimd", (lambda sl, c: lambda s: s.dma_start(
                out=wg_r[:, sl * 2048:(sl + 1) * 2048],
                in_=wTt[:, c * H:(c + 1) * H]))(slot, ci),
               waits=[wg_war[slot]], inc=(f"WG{slot}", 16))
            wg_rdy[slot] = mark(f"WG{slot}")
            op("gpsimd", (lambda sl, c: lambda s: s.dma_start(
                out=wu_r[:, sl * 2048:(sl + 1) * 2048],
                in_=dwTt[:, c * H:(c + 1) * H]))(slot, ci),
               waits=[wu_war[slot]], inc=(f"WU{slot}", 16))
            wu_rdy[slot] = mark(f"WU{slot}")
            op("gpsimd", (lambda sl, c: lambda g: g.tensor_add(
                wsumT[:, c * H:(c + 1) * H],
                wg_r[:, sl * 2048:(sl + 1) * 2048],
                wu_r[:, sl * 2048:(sl + 1) * 2048]))(slot, ci),
               waits=[wg_rdy[slot], wu_rdy[slot], state["sd_all"], h_war[0]]
                     + big_attn_war[0] + xg_war[0] + xg_war[1],
               inc=("SG", 1))
            wg_war[slot] = mark("SG")
            wu_war[slot] = mark("SG")
            if ci == HC - 1:
                big_attn_war[0] = []
                state["wsum_rdy"] = mark("SG")

        def attn_sublayer():
            wsum_rdy = state["wsum_rdy"]
            for tc in range(NC_TOK):
                slot = tc % 2
                norm_rdy = ln_chunk(stream, tc)
                fm_rdy, _ = fm_chunk(tc, norm_rdy)
                for no in range(NQ):
                    di = state["di"]
                    d_b = pd[di]
                    for ci in range(HC):
                        waits = []
                        if ci == 0:
                            waits = [wsum_rdy, fm_rdy, pd_war[di]]
                        inc = ("SP", 1) if ci == HC - 1 else None
                        op("tensor", (lambda db, sl, cc, nn: lambda te: te.matmul(
                            db[:],
                            x_fm_r[:, sl * H + cc * 128: sl * H + (cc + 1) * 128],
                            wsumT[:, cc * H + nn * 512: cc * H + (nn + 1) * 512],
                            start=(cc == 0), stop=(cc == HC - 1)))(d_b, slot, ci, no),
                           waits=waits, inc=inc)
                    d_done = mark("SP")
                    if no == NQ - 1:
                        xfm_war[slot].append(d_done)
                    op("vector", (lambda db, sl, nn: lambda v: v.tensor_add(
                        o_t[:, sl * H + nn * 512: sl * H + (nn + 1) * 512], db[:],
                        x_tm[:, sl * H + nn * 512: sl * H + (nn + 1) * 512]))(d_b, slot, no),
                       waits=[d_done, o_war[slot]], inc=("SV", 1))
                    pd_war[di] = mark("SV")
                    state["di"] ^= 1
                ev_done = mark("SV")
                xt_war[slot] = xt_war[slot] + [ev_done]
                op("sync", (lambda sl, r: lambda e: e.dma_start(
                    out=stream[r:r + P], in_=o_t[:, sl * H:(sl + 1) * H]))(slot, tc * P),
                   waits=[ev_done], inc=("SO", 16))
                o_war[slot] = mark("SO")
                store_marks[tc] = mark("SO")
            m = mark("SP")
            xg_war[0] = xg_war[0] + [m]
            xg_war[1] = xg_war[1] + [m]
            h_war[0] = m

        sweep = 0
        for si, kind in enumerate(SUBS):
            src = x_in if si == 0 else stream
            dst = y_out if si == len(SUBS) - 1 else stream
            if kind == "f":
                nxt_attn = si + 1 < len(SUBS) and SUBS[si + 1] == "a"

                def hook2(tc):
                    if tc < HC // 2:
                        attn_prep_step(2 * tc)
                        attn_prep_step(2 * tc + 1)
                ffn_sublayer(src, dst, sweep,
                             chunk_hook=hook2 if nxt_attn else None)
                sweep += 1
            else:
                attn_sublayer()

        total_so = cnt["SO"]
        op("sync", lambda e: None, waits=[("SO", total_so)])

        def emit(engine_name):
            def run(eng):
                waited = {}
                for (en, fn, ws, incs) in sched:
                    if en != engine_name:
                        continue
                    for (s, v) in ws:
                        if v > waited.get(s, 0):
                            eng.wait_ge(sems[s], v)
                            waited[s] = v
                    ins = fn(eng)
                    for (s, n) in incs:
                        ins.then_inc(sems[s], n)
            return run

        block.vector(emit("vector"))
        block.scalar(emit("scalar"))
        block.tensor(emit("tensor"))
        block.gpsimd(emit("gpsimd"))
        block.sync(emit("sync"))
    return nc


def _tile_inputs(L0_attn_W, L0_attn_delta_W, L0_ffn_router_w, L0_ffn_w_gate,
                 L0_ffn_w_up, L0_ffn_w_down):
    f32 = np.float32
    wg = np.asarray(L0_ffn_w_gate, dtype=f32)
    wu = np.asarray(L0_ffn_w_up, dtype=f32)
    wd = np.asarray(L0_ffn_w_down, dtype=f32)
    wg_t = np.ascontiguousarray(
        wg.reshape(E, HC, P, IC, 128).transpose(0, 3, 2, 1, 4).reshape(E, IC, P, HC * 128))
    wu_t = np.ascontiguousarray(
        wu.reshape(E, HC, P, IC, 128).transpose(0, 3, 2, 1, 4).reshape(E, IC, P, HC * 128))
    wd_t = np.ascontiguousarray(
        wd.reshape(E, IC, P, NQ, 512).transpose(0, 3, 2, 1, 4).reshape(E, NQ, P, IC * 512))
    WT = np.ascontiguousarray(np.asarray(L0_attn_W, dtype=f32).T)
    dWT = np.ascontiguousarray(np.asarray(L0_attn_delta_W, dtype=f32).T)
    wTt = np.ascontiguousarray(WT.reshape(HC, P, H).transpose(1, 0, 2).reshape(P, HC * H))
    dwTt = np.ascontiguousarray(dWT.reshape(HC, P, H).transpose(1, 0, 2).reshape(P, HC * H))
    rwT = np.ascontiguousarray(np.asarray(L0_ffn_router_w, dtype=f32).T)
    rwTt = np.ascontiguousarray(rwT.reshape(HC, P, E).transpose(1, 0, 2).reshape(P, HC * E))
    idn = np.eye(P, dtype=np.float32).astype(ml_dtypes.bfloat16)
    lt = np.triu(np.ones((P, P), dtype=np.float32), 1)
    lt_in = np.concatenate([lt, np.ones((P, P), dtype=np.float32)],
                           axis=1).astype(ml_dtypes.bfloat16)
    ecc_in = np.broadcast_to(
        (np.arange(E, dtype=np.float32) * C)[None, :], (P, E)).copy()
    return dict(wg_t=wg_t, wu_t=wu_t, wd_t=wd_t, wTt=wTt, dwTt=dwTt,
                rwTt=rwTt, idn_in=idn, lt_in=lt_in, ecc_in=ecc_in)


def kernel(L0_attn_x, L0_attn_W, L0_attn_target_W, L0_attn_delta_W,
           L0_ffn_router_w, L0_ffn_w_gate, L0_ffn_w_up, L0_ffn_w_down):
    assert np.asarray(L0_attn_target_W).shape == (H, H)
    x = np.ascontiguousarray(np.asarray(L0_attn_x, dtype=np.float32).reshape(16384, H))
    shared = _tile_inputs(L0_attn_W, L0_attn_delta_W, L0_ffn_router_w,
                          L0_ffn_w_gate, L0_ffn_w_up, L0_ffn_w_down)
    nc = build_nc()
    in_maps = [dict(shared, x_in=np.ascontiguousarray(x[c * NT:(c + 1) * NT]))
               for c in range(NCORES)]
    res = run_bass_kernel_spmd(nc, in_maps, core_ids=list(range(NCORES)))
    y = np.concatenate([np.asarray(res.results[c]["y"]) for c in range(NCORES)], axis=0)
    return y.reshape(4, 4096, H).astype(np.float32)


# version 11
# speedup vs baseline: 1.2220x; 1.1074x over previous
import numpy as np
import ml_dtypes
from contextlib import ExitStack
import concourse.bass as bass
import concourse.mybir as mybir
from concourse.bass_utils import run_bass_kernel_spmd
from concourse.bass_core_rust import IndirectOffsetOnAxis

F32 = mybir.dt.float32
BF16 = mybir.dt.bfloat16
U32 = mybir.dt.uint32
AF = mybir.ActivationFunctionType
ALU = mybir.AluOpType
AX = mybir.AxisListType

P = 128
NCORES = 8
H = 2048
HC = H // P
I = 768
IC = I // P
NQ = 4
E = 32
K = 8
C = 768
ST = C // P
NSUB = 2
TSUB = C // NSUB
S_TOT = E * C
NC_TOK = 16
NT = NC_TOK * P
EPS = 1e-5
INV_H = 1.0 / H

SUBS = ("f", "a", "f")


def build_nc():
    nc = bass.Bass("TRN2", target_bir_lowering=False)
    dram = nc.dram_tensor
    x_in = dram("x_in", [NT, H], F32, kind="ExternalInput").ap()
    wg_t = dram("wg_t", [E, IC, P, HC * 128], F32, kind="ExternalInput").ap()
    wu_t = dram("wu_t", [E, IC, P, HC * 128], F32, kind="ExternalInput").ap()
    wd_t = dram("wd_t", [E, NQ, P, IC * 512], F32, kind="ExternalInput").ap()
    wTt = dram("wTt", [P, HC * H], F32, kind="ExternalInput").ap()
    dwTt = dram("dwTt", [P, HC * H], F32, kind="ExternalInput").ap()
    rwTt = dram("rwTt", [P, HC * E], F32, kind="ExternalInput").ap()
    idn_in = dram("idn_in", [P, P], BF16, kind="ExternalInput").ap()
    lt_in = dram("lt_in", [P, 2 * P], BF16, kind="ExternalInput").ap()
    ecc_in = dram("ecc_in", [P, E], F32, kind="ExternalInput").ap()
    y_out = dram("y", [NT, H], F32, kind="ExternalOutput").ap()
    stream = dram("stream", [NT, H], F32, kind="Internal").ap()
    xslots = dram("xslots", [S_TOT, H], BF16, kind="Internal").ap()
    dall = dram("dall", [S_TOT, H], BF16, kind="Internal").ap()

    es = ExitStack()
    with es:
        sb = lambda name, shape, dt: es.enter_context(nc.sbuf_tensor(name, shape, dt))
        pst = lambda name, shape, dt: es.enter_context(nc.psum_tensor(name, shape, dt))
        big = sb("big", [P, 38912], BF16)
        xg = [big[:, 0:12288], big[:, 12288:24576]]
        h_sl = big[:, 24576:29184]
        o_sl = [big[:, 29184:31232], big[:, 31232:33280]]
        comb = [big[:, 33280:35328], big[:, 35328:37376]]
        wsumT = big[:, 0:32768]

        wg_r = sb("wg_r", [P, 3 * 2048], BF16)
        wu_r = sb("wu_r", [P, 3 * 2048], BF16)
        wd_r = sb("wd_r", [P, 6 * 3072], BF16)
        xsl = sb("xsl", [P, 2 * H], BF16)
        x_tm = sb("x_tm", [P, 2 * H], F32)
        stage = sb("stage", [P, 2 * H], BF16)
        x_fm_r = sb("x_fm_r", [P, 2 * H], BF16)
        o_t = sb("o_t", [P, 2 * H], F32)
        sg = sb("sg", [P, 2 * 512], F32)
        lg = sb("lg", [P, E], F32)
        w_sb = sb("w_sb", [P, E], F32)
        mask_f = sb("mask_f", [P, E], F32)
        mask_b = sb("mask_b", [P, NC_TOK * E], BF16)
        pos_sb = sb("pos_sb", [P, E], F32)
        excl = sb("excl", [P, E], F32)
        slotid = sb("slotid", [P, E], F32)
        eqk = sb("eqk", [P, E], F32)
        selk = sb("selk", [P, E], F32)
        tmpk = sb("tmpk", [P, E], F32)
        ones32 = sb("ones32", [P, E], F32)
        incl32 = sb("incl32", [P, E], F32)
        sk_f8 = sb("sk_f8", [P, K], F32)
        excl_m = sb("excl_m", [P, E], F32)
        sk_u = sb("sk_u", [P, NC_TOK * K], U32)
        g_k = sb("g_k", [P, NC_TOK * K], F32)
        esb = sb("esb", [P, E], F32)
        zap = sb("zap", [P, E], F32)
        wsel = sb("wsel", [P, E], F32)
        m8 = sb("m8", [P, 8], F32)
        rwT_sb = sb("rwT_sb", [P, HC * E], BF16)
        idn = sb("idn", [P, P], BF16)
        ltb = sb("ltb", [P, 2 * P], BF16)
        ecc = sb("ecc", [P, E], F32)
        st_s = sb("st_s", [P, 2 * NC_TOK], F32)
        st_q = sb("st_q", [P, 2 * NC_TOK], F32)
        st_mean = sb("st_mean", [P, 2 * NC_TOK], F32)
        st_m2 = sb("st_m2", [P, 2 * NC_TOK], F32)
        st_var = sb("st_var", [P, 2 * NC_TOK], F32)
        st_ln = sb("st_ln", [P, 2 * NC_TOK], F32)
        st_rstd = sb("st_rstd", [P, 2 * NC_TOK], F32)
        st_negmr = sb("st_negmr", [P, 2 * NC_TOK], F32)
        st_nmax = sb("st_nmax", [P, NC_TOK], F32)
        st_rs = sb("st_rs", [P, NC_TOK], F32)
        st_rr = sb("st_rr", [P, NC_TOK], F32)
        eps_t = sb("eps_t", [P, 1], F32)

        pg = [pst("pg0", [P, 512], F32), pst("pg1", [P, 512], F32)]
        pu = [pst("pu0", [P, 512], F32), pst("pu1", [P, 512], F32)]
        pd = [pst("pd0", [P, 512], F32), pst("pd1", [P, 512], F32)]
        pt = [pst("pt0", [P, 1024], BF16), pst("pt1", [P, 1024], BF16)]

        sem_names = (["SP", "SV", "SA", "SG", "GI", "SI", "SO", "XT0", "XT1",
                      "DS", "DG", "SD", "XS"]
                     + [f"WG{i}" for i in range(3)] + [f"WU{i}" for i in range(3)]
                     + [f"WD{i}" for i in range(6)])
        sems = {n: es.enter_context(nc.semaphore(n)) for n in sem_names}
        block = es.enter_context(nc.Block())

        sched = []
        cnt = {n: 0 for n in sem_names}
        _self_sem = {"vector": "SV", "scalar": "SA"}

        def op(engine, fn, waits=(), inc=None):
            best = {}
            ss = _self_sem.get(engine)
            if ss is None and engine == "gpsimd" and inc is not None and inc[0] == "SG":
                ss = "SG"
            waits = list(waits) + ([(ss, cnt[ss])] if ss else [])
            for w in waits:
                if w is None:
                    continue
                s, v = w
                if v > 0 and v > best.get(s, 0):
                    best[s] = v
            incs = []
            if inc is not None:
                cnt[inc[0]] += inc[1]
                incs.append(inc)
            sched.append((engine, fn, list(best.items()), incs))

        def mark(sem):
            return (sem, cnt[sem])

        wg_war = [None] * 3
        wu_war = [None] * 3
        wd_war = [None] * 6
        wg_rdy = [None] * 3
        wu_rdy = [None] * 3
        wd_rdy = [None] * 6
        xt_war = [[], []]
        xt_rdy = [None, None]
        pg_war = [None, None]
        pu_war = [None, None]
        pd_war = [None, None]
        pt_war = [None, None]
        sg_war = [None, None]
        stage_war = [[], []]
        xfm_war = [[], []]
        xsl_war = [[], []]
        xg_war = [[], []]
        osl_war = [[], []]
        comb_war = [[], []]
        acc_war = [[], []]
        h_war = [None]
        o_war = [None, None]
        esb_war = [None]
        big_attn_war = [[]]
        state = {"gi": 0, "di": 0, "pti": 0, "xslslot": 0,
                 "ds_all": None, "sd_all": None, "dg_all": None, "xs_all": None}
        store_marks = [None] * NC_TOK

        op("sync", lambda e: e.dma_start(out=idn[:], in_=idn_in), inc=("SI", 16))
        op("sync", lambda e: e.dma_start(out=ltb[:], in_=lt_in), inc=("SI", 16))
        op("sync", lambda e: e.dma_start(out=ecc[:], in_=ecc_in), inc=("SI", 16))
        op("gpsimd", lambda g: g.dma_start(out=rwT_sb[:], in_=rwTt), inc=("GI", 16))
        op("vector", lambda v: v.memset(eps_t[:], EPS), inc=("SV", 1))
        op("vector", lambda v: v.memset(ones32[:], 1.0), inc=("SV", 1))
        idn_rdy = mark("SI")
        rw_rdy = mark("GI")
        eps_rdy = ("SV", 1)
        ones_rdy = ("SV", 2)

        def load_wg(e_, m, seq):
            slot = seq % 3
            op("gpsimd", (lambda sl, ee, mm: lambda s: s.dma_start(
                out=wg_r[:, sl * 2048:(sl + 1) * 2048], in_=wg_t[ee, mm]))(slot, e_, m),
               waits=[wg_war[slot]], inc=(f"WG{slot}", 16))
            wg_rdy[slot] = mark(f"WG{slot}")

        def load_wu(e_, m, seq):
            slot = seq % 3
            op("gpsimd", (lambda sl, ee, mm: lambda s: s.dma_start(
                out=wu_r[:, sl * 2048:(sl + 1) * 2048], in_=wu_t[ee, mm]))(slot, e_, m),
               waits=[wu_war[slot]], inc=(f"WU{slot}", 16))
            wu_rdy[slot] = mark(f"WU{slot}")

        def load_wd(e_, q, seq):
            slot = seq % 6
            op("gpsimd", (lambda sl, ee, qq: lambda s: s.dma_start(
                out=wd_r[:, sl * 3072:(sl + 1) * 3072], in_=wd_t[ee, qq]))(slot, e_, q),
               waits=[wd_war[slot]], inc=(f"WD{slot}", 16))
            wd_rdy[slot] = mark(f"WD{slot}")

        def ln_chunk(src, tc, stoff=0):
            slot = tc % 2
            tco = tc + stoff
            stg_v = stage[:, slot * H:(slot + 1) * H]
            op("sync", (lambda sl, r: lambda e: e.dma_start(
                out=x_tm[:, sl * H:(sl + 1) * H], in_=src[r:r + P]))(slot, tc * P),
               waits=xt_war[slot] + [store_marks[tc]], inc=(f"XT{slot}", 16))
            xt_rdy[slot] = mark(f"XT{slot}")
            op("vector", (lambda sl, t: lambda v: v.reduce_sum(
                st_s[:, t:t + 1], x_tm[:, sl * H:(sl + 1) * H], axis=AX.X))(slot, tco),
               waits=[xt_rdy[slot]], inc=("SV", 1))
            s_rdy = mark("SV")
            op("scalar", (lambda sl, t, sv: lambda s: s.activation(
                sv, x_tm[:, sl * H:(sl + 1) * H], AF.Square,
                accum_out=st_q[:, t:t + 1]))(slot, tco, stg_v),
               waits=[xt_rdy[slot]] + stage_war[slot], inc=("SA", 1))
            q_rdy = mark("SA")
            op("vector", (lambda t: lambda v: v.tensor_scalar_mul(
                st_mean[:, t:t + 1], st_s[:, t:t + 1], INV_H))(tco),
               waits=[s_rdy], inc=("SV", 1))
            op("vector", (lambda t: lambda v: v.tensor_mul(
                st_m2[:, t:t + 1], st_mean[:, t:t + 1], st_mean[:, t:t + 1]))(tco),
               waits=[mark("SV")], inc=("SV", 1))
            op("vector", (lambda t: lambda v: v.scalar_tensor_tensor(
                out=st_var[:, t:t + 1], in0=st_q[:, t:t + 1], scalar=INV_H,
                in1=st_m2[:, t:t + 1], op0=ALU.mult, op1=ALU.subtract))(tco),
               waits=[q_rdy, mark("SV")], inc=("SV", 1))
            var_rdy = mark("SV")
            op("scalar", (lambda t: lambda s: s.activation(
                st_ln[:, t:t + 1], st_var[:, t:t + 1], AF.Ln, bias=eps_t[:]))(tco),
               waits=[var_rdy, eps_rdy], inc=("SA", 1))
            op("scalar", (lambda t: lambda s: s.activation(
                st_rstd[:, t:t + 1], st_ln[:, t:t + 1], AF.Exp, scale=-0.5))(tco),
               waits=[mark("SA")], inc=("SA", 1))
            rstd_rdy = mark("SA")
            op("vector", (lambda t: lambda v: v.scalar_tensor_tensor(
                out=st_negmr[:, t:t + 1], in0=st_mean[:, t:t + 1], scalar=-1.0,
                in1=st_rstd[:, t:t + 1], op0=ALU.mult, op1=ALU.mult))(tco),
               waits=[rstd_rdy, mark("SV")], inc=("SV", 1))
            nm_rdy = mark("SV")
            op("scalar", (lambda sl, t, sv: lambda s: s.activation(
                sv, x_tm[:, sl * H:(sl + 1) * H], AF.Identity,
                bias=st_negmr[:, t:t + 1], scale=st_rstd[:, t:t + 1]))(slot, tco, stg_v),
               waits=[nm_rdy, mark("SA")], inc=("SA", 1))
            norm_rdy = mark("SA")
            xt_war[slot] = [norm_rdy, s_rdy]
            return norm_rdy

        def fm_chunk(tc, norm_rdy):
            slot = tc % 2
            evs = []
            for g_ in range(2):
                pti = state["pti"]
                for j in range(8):
                    c = 8 * g_ + j
                    waits = []
                    if j == 0:
                        waits = [norm_rdy, pt_war[pti], idn_rdy]
                    inc = ("SP", 1) if j == 7 else None
                    op("tensor", (lambda pi, sl, cc, jj: lambda t: t.transpose(
                        pt[pi][:, jj * 128:(jj + 1) * 128],
                        stage[:, sl * H + cc * 128: sl * H + (cc + 1) * 128],
                        idn[:]))(pti, slot, c, j),
                       waits=waits, inc=inc)
                tr_done = mark("SP")
                op("vector", (lambda pi, sl, g2: lambda v: v.tensor_copy(
                    x_fm_r[:, sl * H + g2 * 1024: sl * H + (g2 + 1) * 1024],
                    pt[pi][:, 0:1024]))(pti, slot, g_),
                   waits=[tr_done] + xfm_war[slot], inc=("SV", 1))
                pt_war[pti] = mark("SV")
                evs.append(mark("SV"))
                state["pti"] ^= 1
            stage_war[slot] = [tr_done]
            xfm_war[slot] = []
            return evs[-1], tr_done

        def route_chunk(tc, fm_rdy, sub_base):
            slot = tc % 2
            for c in range(HC):
                waits = []
                if c == 0:
                    waits = [fm_rdy, rw_rdy, pg_war[0]]
                inc = ("SP", 1) if c == HC - 1 else None
                op("tensor", (lambda sl, cc: lambda te: te.matmul(
                    pg[0][:, 0:E],
                    x_fm_r[:, sl * H + cc * 128: sl * H + (cc + 1) * 128],
                    rwT_sb[:, cc * E:(cc + 1) * E],
                    start=(cc == 0), stop=(cc == HC - 1)))(slot, c),
                   waits=waits, inc=inc)
            lg_rdy = mark("SP")
            xfm_war[slot].append(lg_rdy)
            op("vector", lambda v: v.tensor_copy(lg[:], pg[0][:, 0:E]),
               waits=[lg_rdy], inc=("SV", 1))
            pg_war[0] = mark("SV")
            op("vector", (lambda t: lambda v: v.reduce_max(
                st_nmax[:, t:t + 1], lg[:], axis=AX.X, negate=True))(tc),
               waits=[mark("SV"), mark("SA")], inc=("SV", 1))
            op("scalar", (lambda t: lambda s: s.activation(
                esb[:], lg[:], AF.Exp, bias=st_nmax[:, t:t + 1]))(tc),
               waits=[mark("SV"), esb_war[0]], inc=("SA", 1))
            e_rdy = mark("SA")
            op("vector", lambda v: v.max(out=m8[:], in_=esb[:]),
               waits=[e_rdy, mark("SV")], inc=("SV", 1))
            op("vector", lambda v: v.match_replace(
                out=zap[:], in_to_replace=m8[:], in_values=esb[:], imm_value=0.0),
               waits=[mark("SV")], inc=("SV", 1))
            op("vector", lambda v: v.tensor_sub(wsel[:], esb[:], zap[:]),
               waits=[mark("SV")], inc=("SV", 1))
            esb_war[0] = mark("SV")
            op("vector", (lambda t: lambda v: v.reduce_sum(
                st_rs[:, t:t + 1], wsel[:], axis=AX.X))(tc),
               waits=[mark("SV")], inc=("SV", 1))
            op("vector", (lambda t: lambda v: v.reciprocal(
                st_rr[:, t:t + 1], st_rs[:, t:t + 1]))(tc),
               waits=[mark("SV")], inc=("SV", 1))
            op("vector", (lambda t: lambda v: v.tensor_scalar_mul(
                w_sb[:], wsel[:], st_rr[:, t:t + 1]))(tc),
               waits=[mark("SV")], inc=("SV", 1))
            op("vector", lambda v: v.tensor_scalar(
                out=mask_f[:], in0=w_sb[:], scalar1=0.0, scalar2=None, op0=ALU.is_gt),
               waits=[mark("SV")], inc=("SV", 1))
            mf_rdy = mark("SV")
            op("vector", (lambda t: lambda v: v.tensor_copy(
                mask_b[:, t * E:(t + 1) * E], mask_f[:]))(tc),
               waits=[mf_rdy], inc=("SV", 1))
            mb_rdy = mark("SV")
            op("vector", lambda v: v.tensor_tensor_scan(
                out=incl32[:], data0=ones32[:], data1=mask_f[:],
                initial=100.0, op0=ALU.mult, op1=ALU.add),
               waits=[mf_rdy, ones_rdy], inc=("SV", 1))
            op("vector", lambda v: v.scalar_tensor_tensor(
                out=excl_m[:], in0=mask_f[:], scalar=-101.0, in1=incl32[:],
                op0=ALU.mult, op1=ALU.add),
               waits=[mark("SV")], inc=("SV", 1))
            excl_rdy = mark("SV")
            for i in range(tc + 1):
                waits = []
                if i == 0:
                    waits = [mb_rdy, pg_war[1], idn_rdy]
                inc = ("SP", 1) if i == tc else None
                lsl = (0 if i == tc else P)
                op("tensor", (lambda ii, ls: lambda te: te.matmul(
                    pg[1][:, 0:E],
                    ltb[:, ls:ls + P],
                    mask_b[:, ii * E:(ii + 1) * E],
                    start=(ii == 0), stop=(ii == tc)))(i, lsl),
                   waits=waits, inc=inc)
            pos_rdy = mark("SP")
            op("vector", lambda v: v.tensor_copy(pos_sb[:], pg[1][:, 0:E]),
               waits=[pos_rdy], inc=("SV", 1))
            pg_war[1] = mark("SV")
            op("vector", lambda v: v.tensor_add(slotid[:], pos_sb[:], ecc[:]),
               waits=[mark("SV")], inc=("SV", 1))
            sid_rdy = mark("SV")
            for k in range(K):
                op("vector", (lambda kk: lambda v: v.tensor_scalar(
                    out=eqk[:], in0=excl_m[:], scalar1=float(kk),
                    scalar2=None, op0=ALU.is_equal))(k),
                   waits=[excl_rdy, mark("SV")], inc=("SV", 1))
                op("vector", lambda v: v.tensor_mul(tmpk[:], eqk[:], slotid[:]),
                   waits=[sid_rdy, mark("SV")], inc=("SV", 1))
                op("vector", (lambda kk: lambda v: v.reduce_sum(
                    sk_f8[:, kk:kk + 1], tmpk[:], axis=AX.X))(k),
                   waits=[mark("SV")], inc=("SV", 1))
                op("vector", lambda v: v.tensor_mul(tmpk[:], eqk[:], w_sb[:]),
                   waits=[mark("SV")], inc=("SV", 1))
                op("vector", (lambda t, kk: lambda v: v.reduce_sum(
                    g_k[:, t * K + kk: t * K + kk + 1], tmpk[:], axis=AX.X))(tc, k),
                   waits=[mark("SV")], inc=("SV", 1))
            op("vector", (lambda t: lambda v: v.tensor_copy(
                sk_u[:, t * K:(t + 1) * K], sk_f8[:]))(tc),
               waits=[mark("SV")], inc=("SV", 1))
            sku_rdy = mark("SV")
            op("vector", (lambda t: lambda v: v.tensor_scalar_mul(
                g_k[:, t * K:(t + 1) * K], g_k[:, t * K:(t + 1) * K], 0.5))(tc),
               waits=[mark("SV")], inc=("SV", 1))
            for k in range(K):
                op("gpsimd", (lambda sl, t, kk, sbv: lambda g: g.indirect_dma_start(
                    out=xslots,
                    out_offset=IndirectOffsetOnAxis(
                        sk_u[:, t * K + kk: t * K + kk + 1], 0),
                    in_=sbv, in_offset=None))(
                        slot, tc, k, stage[:, slot * H:(slot + 1) * H]),
                   waits=[sku_rdy, sub_base], inc=("DS", 16))
                stage_war[slot] = stage_war[slot] + [mark("DS")]

        def tile_load(e_, st, ds_all):
            xslslot = state["xslslot"]
            state["xslslot"] ^= 1
            op("sync", (lambda ee, ss, xsl_s: lambda e: e.dma_start(
                out=xsl[:, xsl_s * H:(xsl_s + 1) * H],
                in_=xslots[ee * C + ss * P: ee * C + ss * P + P]))(e_, st, xslslot),
               waits=xsl_war[xslslot] + [ds_all], inc=("XS", 16))
            return xslslot, mark("XS")

        def emit_tile(e_, st, rdy, loaded):
            xb = e_ % 2
            xslslot, xsl_rdy = loaded
            tr_done = None
            for g_ in range(2):
                pti = state["pti"]
                for j in range(8):
                    c = 8 * g_ + j
                    waits = []
                    if j == 0:
                        waits = [xsl_rdy, pt_war[pti], idn_rdy]
                    inc = ("SP", 1) if j == 7 else None
                    op("tensor", (lambda pi, xs, cc, jj: lambda t: t.transpose(
                        pt[pi][:, jj * 128:(jj + 1) * 128],
                        xsl[:, xs * H + cc * 128: xs * H + (cc + 1) * 128],
                        idn[:]))(pti, xslslot, c, j),
                       waits=waits, inc=inc)
                tr_done = mark("SP")
                op("vector", (lambda pi, xv, g2, ss: lambda v: v.tensor_copy(
                    xv.rearrange("p (c s) -> p c s", c=HC)[
                        :, g2 * 8:(g2 + 1) * 8, ss * P:(ss + 1) * P],
                    pt[pi][:, 0:1024].rearrange("p (b t) -> p b t", b=8)))(
                        pti, xg[xb], g_, st),
                   waits=[tr_done] + xg_war[xb], inc=("SV", 1))
                pt_war[pti] = mark("SV")
                state["pti"] ^= 1
            xg_war[xb] = []
            xsl_war[xslslot] = [tr_done]
            rdy.append(mark("SV"))

        def attn_prep_step(ci, extra_waits):
            slot = ci % 3
            op("gpsimd", (lambda sl, c: lambda s: s.dma_start(
                out=wg_r[:, sl * 2048:(sl + 1) * 2048],
                in_=wTt[:, c * H:(c + 1) * H]))(slot, ci),
               waits=[wg_war[slot]], inc=(f"WG{slot}", 16))
            wg_rdy[slot] = mark(f"WG{slot}")
            op("gpsimd", (lambda sl, c: lambda s: s.dma_start(
                out=wu_r[:, sl * 2048:(sl + 1) * 2048],
                in_=dwTt[:, c * H:(c + 1) * H]))(slot, ci),
               waits=[wu_war[slot]], inc=(f"WU{slot}", 16))
            wu_rdy[slot] = mark(f"WU{slot}")
            op("gpsimd", (lambda sl, c: lambda g: g.tensor_add(
                wsumT[:, c * H:(c + 1) * H],
                wg_r[:, sl * 2048:(sl + 1) * 2048],
                wu_r[:, sl * 2048:(sl + 1) * 2048]))(slot, ci),
               waits=[wg_rdy[slot], wu_rdy[slot]] + extra_waits,
               inc=("SG", 1))
            wg_war[slot] = mark("SG")
            wu_war[slot] = mark("SG")
            if ci == HC - 1:
                state["wsum_rdy"] = mark("SG")

        def experts(sweep, emit_prep=False):
            ds_all = mark("DS")
            load_wg(0, 0, sweep * 7919 + 0)
            load_wu(0, 0, sweep * 7919 + 0)
            load_wg(0, 1, sweep * 7919 + 1)
            load_wu(0, 1, sweep * 7919 + 1)
            load_wd(0, 0, sweep * 7919 + 0)
            tiles_rdy = []
            for st in range(ST):
                emit_tile(0, st, tiles_rdy, tile_load(0, st, ds_all))
            for e_ in range(E):
                xb = e_ % 2
                tile_rdy = tiles_rdy
                h_last = None
                for m in range(IC):
                    seq = sweep * 7919 + e_ * IC + m
                    slot = seq % 3
                    wgr = wg_rdy[slot]
                    wur = wu_rdy[slot]
                    g_done = None
                    u_done = None
                    for n in range(NSUB):
                        xg_rdy = tile_rdy[3 * n + 2]
                        gi = state["gi"]
                        g_b, u_b = pg[gi], pu[gi]
                        for c in range(HC):
                            waits = []
                            if c == 0:
                                waits = [wgr, xg_rdy, pg_war[gi]]
                            inc = ("SP", 1) if c == HC - 1 else None
                            op("tensor", (lambda gb, sl, cc, nn, xv: lambda t: t.matmul(
                                gb[:, 0:TSUB],
                                wg_r[:, sl * 2048 + cc * 128: sl * 2048 + cc * 128 + 128],
                                xv[:, cc * C + nn * TSUB: cc * C + (nn + 1) * TSUB],
                                start=(cc == 0), stop=(cc == HC - 1)))(g_b, slot, c, n, xg[xb]),
                               waits=waits, inc=inc)
                        g_done = mark("SP")
                        for c in range(HC):
                            waits = []
                            if c == 0:
                                waits = [wur, xg_rdy, pu_war[gi]]
                            inc = ("SP", 1) if c == HC - 1 else None
                            op("tensor", (lambda ub, sl, cc, nn, xv: lambda t: t.matmul(
                                ub[:, 0:TSUB],
                                wu_r[:, sl * 2048 + cc * 128: sl * 2048 + cc * 128 + 128],
                                xv[:, cc * C + nn * TSUB: cc * C + (nn + 1) * TSUB],
                                start=(cc == 0), stop=(cc == HC - 1)))(u_b, slot, c, n, xg[xb]),
                               waits=waits, inc=inc)
                        u_done = mark("SP")
                        sgv = sg[:, gi * 512: gi * 512 + TSUB]
                        op("scalar", (lambda gb, sv: lambda s: s.activation(
                            sv, gb[:, 0:TSUB], AF.Silu))(g_b, sgv),
                           waits=[g_done, sg_war[gi]], inc=("SA", 1))
                        pg_war[gi] = mark("SA")
                        silu_done = mark("SA")
                        hv = h_sl[:, m * C + n * TSUB: m * C + (n + 1) * TSUB]
                        op("vector", (lambda ub, sv, hh: lambda v: v.tensor_mul(
                            hh, sv, ub[:, 0:TSUB]))(u_b, sgv, hv),
                           waits=[u_done, silu_done, h_war[0]], inc=("SV", 1))
                        pu_war[gi] = mark("SV")
                        sg_war[gi] = mark("SV")
                        h_last = mark("SV")
                        state["gi"] ^= 1
                    wg_war[slot] = g_done
                    wu_war[slot] = u_done
                    nm, ne_ = m + 2, e_
                    if nm >= IC:
                        nm -= IC
                        ne_ += 1
                    if ne_ < E:
                        load_wg(ne_, nm, sweep * 7919 + ne_ * IC + nm)
                        load_wu(ne_, nm, sweep * 7919 + ne_ * IC + nm)
                xg_war[xb] = [mark("SP")]
                big_attn_war[0] = big_attn_war[0] + [mark("SP")]
                tiles_rdy = []
                if e_ + 1 < E:
                    state["pend_tile"] = (0, tile_load(e_ + 1, 0, ds_all))
                for st in range(ST):
                    osl = st % 2
                    q_done = []
                    for q in range(NQ):
                        seq = sweep * 7919 + e_ * NQ + q
                        slot = seq % 6
                        wdr = wd_rdy[slot]
                        if st == 0:
                            nq_, ne_ = q + 1, e_
                            if nq_ >= NQ:
                                nq_ -= NQ
                                ne_ += 1
                            if ne_ < E:
                                load_wd(ne_, nq_, sweep * 7919 + ne_ * NQ + nq_)
                        di = state["di"]
                        d_b = pd[di]
                        for c in range(IC):
                            waits = []
                            if c == 0:
                                waits = [wdr, h_last, pd_war[di]]
                            inc = ("SP", 1) if c == IC - 1 else None
                            op("tensor", (lambda db, sl, cc, qq, ss: lambda t: t.matmul(
                                db[:],
                                h_sl[:, cc * C + ss * P: cc * C + (ss + 1) * P],
                                wd_r[:, sl * 3072 + cc * 512: sl * 3072 + (cc + 1) * 512],
                                start=(cc == 0), stop=(cc == IC - 1)))(d_b, slot, c, q, st),
                               waits=waits, inc=inc)
                        d_done = mark("SP")
                        if st == ST - 1:
                            wd_war[slot] = d_done
                        op("vector", (lambda db, ov, qq: lambda v: v.tensor_copy(
                            ov[:, qq * 512:(qq + 1) * 512], db[:]))(d_b, o_sl[osl], q),
                           waits=[d_done] + osl_war[osl], inc=("SV", 1))
                        pd_war[di] = mark("SV")
                        q_done.append(mark("SV"))
                        state["di"] ^= 1
                    osl_war[osl] = []
                    op("sync", (lambda ov, ee, ss: lambda e: e.dma_start(
                        out=dall[ee * C + ss * P: ee * C + ss * P + P], in_=ov))(
                            o_sl[osl], e_, st),
                       waits=[q_done[-1], state["dg_all"]], inc=("SD", 16))
                    osl_war[osl] = [mark("SD")]
                    if e_ + 1 < E:
                        pst_, loaded = state["pend_tile"]
                        emit_tile(e_ + 1, pst_, tiles_rdy, loaded)
                        if pst_ + 1 < ST:
                            state["pend_tile"] = (pst_ + 1,
                                                  tile_load(e_ + 1, pst_ + 1, ds_all))
                h_war[0] = mark("SP")
            if emit_prep:
                xg_free = xg_war[0] + xg_war[1]
                late = [h_war[0], mark("SD")]
                for ci in range(HC):
                    attn_prep_step(ci, xg_free if ci < 12 else late)
            state["sd_all"] = mark("SD")
            state["xs_all"] = mark("XS")

        def combine_chunk(src, dst, tc, sd_all):
            slot = tc % 2
            av = o_t[:, slot * H:(slot + 1) * H]
            op("sync", (lambda sl, r, aa: lambda e: e.dma_start(
                out=aa, in_=src[r:r + P]))(slot, tc * P, av),
               waits=[o_war[slot], store_marks[tc]], inc=(f"XT{slot}", 16))
            x_rdy = mark(f"XT{slot}")
            last_acc = None
            for k in range(K):
                cb = k % 2
                op("gpsimd", (lambda t, kk, cv: lambda g: g.indirect_dma_start(
                    out=cv, out_offset=None,
                    in_=dall,
                    in_offset=IndirectOffsetOnAxis(
                        sk_u[:, t * K + kk: t * K + kk + 1], 0)))(tc, k, comb[cb]),
                   waits=[sd_all] + comb_war[cb], inc=("DG", 16))
                g_rdy = mark("DG")
                gv = g_k[:, tc * K + k: tc * K + k + 1]
                op("vector", (lambda cv, aa, gg: lambda v: v.scalar_tensor_tensor(
                    out=aa, in0=cv, scalar=gg, in1=aa,
                    op0=ALU.mult, op1=ALU.add))(comb[cb], av, gv),
                   waits=[g_rdy] + ([x_rdy] if k == 0 else []), inc=("SV", 1))
                comb_war[cb] = [mark("SV")]
                last_acc = mark("SV")
            op("sync", (lambda sl, r, aa: lambda e: e.dma_start(
                out=dst[r:r + P], in_=aa))(slot, tc * P, av),
               waits=[last_acc], inc=("SO", 16))
            o_war[slot] = mark("SO")
            store_marks[tc] = mark("SO")

        def ffnA_chunk(src, tc, sub_base):
            norm_rdy = ln_chunk(src, tc)
            fm_rdy, _ = fm_chunk(tc, norm_rdy)
            route_chunk(tc, fm_rdy, sub_base)

        def attn_chunk(tc):
            wsum_rdy = state["wsum_rdy"]
            slot = tc % 2
            norm_rdy = ln_chunk(stream, tc, stoff=NC_TOK)
            fm_rdy, _ = fm_chunk(tc, norm_rdy)
            for no in range(NQ):
                di = state["di"]
                d_b = pd[di]
                for ci in range(HC):
                    waits = []
                    if ci == 0:
                        waits = [wsum_rdy, fm_rdy, pd_war[di]]
                    inc = ("SP", 1) if ci == HC - 1 else None
                    op("tensor", (lambda db, sl, cc, nn: lambda te: te.matmul(
                        db[:],
                        x_fm_r[:, sl * H + cc * 128: sl * H + (cc + 1) * 128],
                        wsumT[:, cc * H + nn * 512: cc * H + (nn + 1) * 512],
                        start=(cc == 0), stop=(cc == HC - 1)))(d_b, slot, ci, no),
                       waits=waits, inc=inc)
                d_done = mark("SP")
                if no == NQ - 1:
                    xfm_war[slot].append(d_done)
                op("vector", (lambda db, sl, nn: lambda v: v.tensor_add(
                    o_t[:, sl * H + nn * 512: sl * H + (nn + 1) * 512], db[:],
                    x_tm[:, sl * H + nn * 512: sl * H + (nn + 1) * 512]))(d_b, slot, no),
                   waits=[d_done, o_war[slot]], inc=("SV", 1))
                pd_war[di] = mark("SV")
                state["di"] ^= 1
            ev_done = mark("SV")
            xt_war[slot] = xt_war[slot] + [ev_done]
            op("sync", (lambda sl, r: lambda e: e.dma_start(
                out=stream[r:r + P], in_=o_t[:, sl * H:(sl + 1) * H]))(slot, tc * P),
               waits=[ev_done], inc=("SO", 16))
            o_war[slot] = mark("SO")
            store_marks[tc] = mark("SO")

        for tc in range(NC_TOK):
            ffnA_chunk(x_in, tc, None)
        experts(0, emit_prep=True)
        sd_all = state["sd_all"]
        sub2 = state["xs_all"]
        for step in range(NC_TOK + 1):
            if step < NC_TOK:
                combine_chunk(x_in, stream, step, sd_all)
            if step >= 1:
                attn_chunk(step - 1)
        for tc in range(NC_TOK):
            ffnA_chunk(stream, tc, sub2)
        state["dg_all"] = mark("DG")
        m = mark("SP")
        xg_war[0] = xg_war[0] + [m]
        xg_war[1] = xg_war[1] + [m]
        h_war[0] = m
        experts(1)
        sd_all = state["sd_all"]
        for tc in range(NC_TOK):
            combine_chunk(stream, y_out, tc, sd_all)

        total_so = cnt["SO"]
        op("sync", lambda e: None, waits=[("SO", total_so)])

        def emit(engine_name):
            def run(eng):
                waited = {}
                for (en, fn, ws, incs) in sched:
                    if en != engine_name:
                        continue
                    for (s, v) in ws:
                        if v > waited.get(s, 0):
                            eng.wait_ge(sems[s], v)
                            waited[s] = v
                    ins = fn(eng)
                    for (s, n) in incs:
                        ins.then_inc(sems[s], n)
            return run

        block.vector(emit("vector"))
        block.scalar(emit("scalar"))
        block.tensor(emit("tensor"))
        block.gpsimd(emit("gpsimd"))
        block.sync(emit("sync"))
    return nc


def _tile_inputs(L0_attn_W, L0_attn_delta_W, L0_ffn_router_w, L0_ffn_w_gate,
                 L0_ffn_w_up, L0_ffn_w_down):
    f32 = np.float32
    wg = np.asarray(L0_ffn_w_gate, dtype=f32)
    wu = np.asarray(L0_ffn_w_up, dtype=f32)
    wd = np.asarray(L0_ffn_w_down, dtype=f32)
    wg_t = np.ascontiguousarray(
        wg.reshape(E, HC, P, IC, 128).transpose(0, 3, 2, 1, 4).reshape(E, IC, P, HC * 128))
    wu_t = np.ascontiguousarray(
        wu.reshape(E, HC, P, IC, 128).transpose(0, 3, 2, 1, 4).reshape(E, IC, P, HC * 128))
    wd_t = np.ascontiguousarray(
        wd.reshape(E, IC, P, NQ, 512).transpose(0, 3, 2, 1, 4).reshape(E, NQ, P, IC * 512))
    WT = np.ascontiguousarray(np.asarray(L0_attn_W, dtype=f32).T)
    dWT = np.ascontiguousarray(np.asarray(L0_attn_delta_W, dtype=f32).T)
    wTt = np.ascontiguousarray(WT.reshape(HC, P, H).transpose(1, 0, 2).reshape(P, HC * H))
    dwTt = np.ascontiguousarray(dWT.reshape(HC, P, H).transpose(1, 0, 2).reshape(P, HC * H))
    rwT = np.ascontiguousarray(np.asarray(L0_ffn_router_w, dtype=f32).T)
    rwTt = np.ascontiguousarray(rwT.reshape(HC, P, E).transpose(1, 0, 2).reshape(P, HC * E))
    idn = np.eye(P, dtype=np.float32).astype(ml_dtypes.bfloat16)
    lt = np.triu(np.ones((P, P), dtype=np.float32), 1)
    lt_in = np.concatenate([lt, np.ones((P, P), dtype=np.float32)],
                           axis=1).astype(ml_dtypes.bfloat16)
    ecc_in = np.broadcast_to(
        (np.arange(E, dtype=np.float32) * C)[None, :], (P, E)).copy()
    return dict(wg_t=wg_t, wu_t=wu_t, wd_t=wd_t, wTt=wTt, dwTt=dwTt,
                rwTt=rwTt, idn_in=idn, lt_in=lt_in, ecc_in=ecc_in)


def kernel(L0_attn_x, L0_attn_W, L0_attn_target_W, L0_attn_delta_W,
           L0_ffn_router_w, L0_ffn_w_gate, L0_ffn_w_up, L0_ffn_w_down):
    assert np.asarray(L0_attn_target_W).shape == (H, H)
    x = np.ascontiguousarray(np.asarray(L0_attn_x, dtype=np.float32).reshape(16384, H))
    shared = _tile_inputs(L0_attn_W, L0_attn_delta_W, L0_ffn_router_w,
                          L0_ffn_w_gate, L0_ffn_w_up, L0_ffn_w_down)
    nc = build_nc()
    in_maps = [dict(shared, x_in=np.ascontiguousarray(x[c * NT:(c + 1) * NT]))
               for c in range(NCORES)]
    res = run_bass_kernel_spmd(nc, in_maps, core_ids=list(range(NCORES)))
    y = np.concatenate([np.asarray(res.results[c]["y"]) for c in range(NCORES)], axis=0)
    return y.reshape(4, 4096, H).astype(np.float32)


# version 15 (final)
# speedup vs baseline: 1.3744x; 1.1247x over previous
import numpy as np
import ml_dtypes
from contextlib import ExitStack
import concourse.bass as bass
import concourse.mybir as mybir
from concourse.bass_utils import run_bass_kernel_spmd
from concourse.bass_core_rust import IndirectOffsetOnAxis

F32 = mybir.dt.float32
BF16 = mybir.dt.bfloat16
U32 = mybir.dt.uint32
AF = mybir.ActivationFunctionType
ALU = mybir.AluOpType
AX = mybir.AxisListType

P = 128
NCORES = 8
H = 2048
HC = H // P
I = 768
IC = I // P
NQ = 4
E = 32
K = 8
C = 640
ST = C // P
NSUB = 2
TSUB = C // NSUB
S_TOT = E * C
NC_TOK = 16
NT = NC_TOK * P
EPS = 1e-5
INV_H = 1.0 / H

SUBS = ("f", "a", "f")


def build_nc():
    nc = bass.Bass("TRN2", target_bir_lowering=False)
    dram = nc.dram_tensor
    x_in = dram("x_in", [NT, H], F32, kind="ExternalInput").ap()
    wg_t = dram("wg_t", [E, IC, P, HC * 128], F32, kind="ExternalInput").ap()
    wu_t = dram("wu_t", [E, IC, P, HC * 128], F32, kind="ExternalInput").ap()
    wd_t = dram("wd_t", [E, NQ, P, IC * 512], F32, kind="ExternalInput").ap()
    wTt = dram("wTt", [P, HC * H], F32, kind="ExternalInput").ap()
    dwTt = dram("dwTt", [P, HC * H], F32, kind="ExternalInput").ap()
    rwTt = dram("rwTt", [P, HC * E], F32, kind="ExternalInput").ap()
    idn_in = dram("idn_in", [P, P], BF16, kind="ExternalInput").ap()
    lt_in = dram("lt_in", [P, 2 * P], BF16, kind="ExternalInput").ap()
    ecc_in = dram("ecc_in", [P, E], F32, kind="ExternalInput").ap()
    y_out = dram("y", [NT, H], F32, kind="ExternalOutput").ap()
    stream = dram("stream", [NT, H], F32, kind="Internal").ap()
    xslots = dram("xslots", [S_TOT, H], BF16, kind="Internal").ap()
    dall = dram("dall", [S_TOT, H], BF16, kind="Internal").ap()

    es = ExitStack()
    with es:
        sb = lambda name, shape, dt: es.enter_context(nc.sbuf_tensor(name, shape, dt))
        pst = lambda name, shape, dt: es.enter_context(nc.psum_tensor(name, shape, dt))
        XGW = HC * C
        HW_ = IC * C
        OB = 2 * XGW + HW_
        CB = max(2 * XGW + HW_ + 2 * H, HC * H)
        big = sb("big", [P, CB + 2 * H], BF16)
        xg = [big[:, 0:XGW], big[:, XGW:2 * XGW]]
        h_sl = big[:, 2 * XGW:2 * XGW + HW_]
        o_sl = [big[:, OB:OB + H], big[:, OB + H:OB + 2 * H]]
        comb = [big[:, CB:CB + H], big[:, CB + H:CB + 2 * H]]
        wsumT = big[:, 0:HC * H]

        wg_r = sb("wg_r", [P, 3 * 2048], BF16)
        wu_r = sb("wu_r", [P, 3 * 2048], BF16)
        wd_r = sb("wd_r", [P, 6 * 3072], BF16)
        xsl = sb("xsl", [P, 2 * H], BF16)
        x_tm = sb("x_tm", [P, 2 * H], F32)
        stage = sb("stage", [P, 2 * H], BF16)
        x_fm_r = sb("x_fm_r", [P, 2 * H], BF16)
        o_t = sb("o_t", [P, 2 * H], F32)
        sg = sb("sg", [P, 2 * TSUB], F32)
        lg = sb("lg", [P, E], F32)
        w_sb = sb("w_sb", [P, E], F32)
        mask_f = sb("mask_f", [P, E], F32)
        mask_b = sb("mask_b", [P, NC_TOK * E], BF16)
        pos_sb = sb("pos_sb", [P, E], F32)
        excl = sb("excl", [P, E], F32)
        slotid = sb("slotid", [P, E], F32)
        eqk = sb("eqk", [P, E], F32)
        selk = sb("selk", [P, E], F32)
        tmpk = sb("tmpk", [P, E], F32)
        ones32 = sb("ones32", [P, E], F32)
        incl32 = sb("incl32", [P, E], F32)
        sk_f8 = sb("sk_f8", [P, K], F32)
        excl_m = sb("excl_m", [P, E], F32)
        sk_u = sb("sk_u", [P, NC_TOK * K], U32)
        g_k = sb("g_k", [P, NC_TOK * K], F32)
        esb = sb("esb", [P, E], F32)
        zap = sb("zap", [P, E], F32)
        wsel = sb("wsel", [P, E], F32)
        m8 = sb("m8", [P, 8], F32)
        rwT_sb = sb("rwT_sb", [P, HC * E], BF16)
        idn = sb("idn", [P, P], BF16)
        ltb = sb("ltb", [P, 2 * P], BF16)
        ecc = sb("ecc", [P, E], F32)
        st_s = sb("st_s", [P, 2 * NC_TOK], F32)
        st_q = sb("st_q", [P, 2 * NC_TOK], F32)
        st_mean = sb("st_mean", [P, 2 * NC_TOK], F32)
        st_m2 = sb("st_m2", [P, 2 * NC_TOK], F32)
        st_var = sb("st_var", [P, 2 * NC_TOK], F32)
        st_ln = sb("st_ln", [P, 2 * NC_TOK], F32)
        st_rstd = sb("st_rstd", [P, 2 * NC_TOK], F32)
        st_negmr = sb("st_negmr", [P, 2 * NC_TOK], F32)
        st_nmax = sb("st_nmax", [P, NC_TOK], F32)
        st_rs = sb("st_rs", [P, NC_TOK], F32)
        st_rr = sb("st_rr", [P, NC_TOK], F32)
        eps_t = sb("eps_t", [P, 1], F32)

        pg = [pst("pg0", [P, 512], F32), pst("pg1", [P, 512], F32)]
        pu = [pst("pu0", [P, 512], F32), pst("pu1", [P, 512], F32)]
        pd = [pst("pd0", [P, 512], F32), pst("pd1", [P, 512], F32)]
        pt = [pst("pt0", [P, 1024], BF16), pst("pt1", [P, 1024], BF16)]

        sem_names = (["SP", "SV", "SA", "SG", "GI", "SI", "SO", "XT0", "XT1",
                      "DS", "DG", "SD", "XS"]
                     + [f"WG{i}" for i in range(3)] + [f"WU{i}" for i in range(3)]
                     + [f"WD{i}" for i in range(6)])
        sems = {n: es.enter_context(nc.semaphore(n)) for n in sem_names}
        block = es.enter_context(nc.Block())

        sched = []
        cnt = {n: 0 for n in sem_names}
        _self_sem = {"vector": "SV", "scalar": "SA"}

        def op(engine, fn, waits=(), inc=None):
            best = {}
            ss = _self_sem.get(engine)
            if ss is None and engine == "gpsimd" and inc is not None and inc[0] == "SG":
                ss = "SG"
            waits = list(waits) + ([(ss, cnt[ss])] if ss else [])
            for w in waits:
                if w is None:
                    continue
                s, v = w
                if v > 0 and v > best.get(s, 0):
                    best[s] = v
            incs = []
            if inc is not None:
                cnt[inc[0]] += inc[1]
                incs.append(inc)
            sched.append((engine, fn, list(best.items()), incs))

        def mark(sem):
            return (sem, cnt[sem])

        wg_war = [None] * 3
        wu_war = [None] * 3
        wd_war = [None] * 6
        wg_rdy = [None] * 3
        wu_rdy = [None] * 3
        wd_rdy = [None] * 6
        xt_war = [[], []]
        xt_rdy = [None, None]
        pg_war = [None, None]
        pu_war = [None, None]
        pd_war = [None, None]
        pt_war = [None, None]
        sg_war = [None, None]
        stage_war = [[], []]
        xfm_war = [[], []]
        xsl_war = [[], []]
        xg_war = [[], []]
        osl_war = [[], []]
        comb_war = [[], [], [], []]
        acc_war = [[], []]
        h_war = [None]
        o_war = [None, None]
        esb_war = [None]
        big_attn_war = [[]]
        state = {"gi": 0, "di": 0, "pti": 0, "xslslot": 0,
                 "ds_all": None, "sd_all": None, "dg_all": None, "xs_all": None}
        store_marks = [None] * NC_TOK

        op("sync", lambda e: e.dma_start(out=idn[:], in_=idn_in), inc=("SI", 16))
        op("sync", lambda e: e.dma_start(out=ltb[:], in_=lt_in), inc=("SI", 16))
        op("sync", lambda e: e.dma_start(out=ecc[:], in_=ecc_in), inc=("SI", 16))
        op("gpsimd", lambda g: g.dma_start(out=rwT_sb[:], in_=rwTt), inc=("GI", 16))
        op("vector", lambda v: v.memset(eps_t[:], EPS), inc=("SV", 1))
        op("vector", lambda v: v.memset(ones32[:], 1.0), inc=("SV", 1))
        idn_rdy = mark("SI")
        rw_rdy = mark("GI")
        eps_rdy = ("SV", 1)
        ones_rdy = ("SV", 2)

        def load_wg(e_, m, seq):
            slot = seq % 3
            op("gpsimd", (lambda sl, ee, mm: lambda s: s.dma_start(
                out=wg_r[:, sl * 2048:(sl + 1) * 2048], in_=wg_t[ee, mm]))(slot, e_, m),
               waits=[wg_war[slot]], inc=(f"WG{slot}", 16))
            wg_rdy[slot] = mark(f"WG{slot}")

        def load_wu(e_, m, seq):
            slot = seq % 3
            op("gpsimd", (lambda sl, ee, mm: lambda s: s.dma_start(
                out=wu_r[:, sl * 2048:(sl + 1) * 2048], in_=wu_t[ee, mm]))(slot, e_, m),
               waits=[wu_war[slot]], inc=(f"WU{slot}", 16))
            wu_rdy[slot] = mark(f"WU{slot}")

        def load_wd(e_, q, seq):
            slot = seq % 6
            op("gpsimd", (lambda sl, ee, qq: lambda s: s.dma_start(
                out=wd_r[:, sl * 3072:(sl + 1) * 3072], in_=wd_t[ee, qq]))(slot, e_, q),
               waits=[wd_war[slot]], inc=(f"WD{slot}", 16))
            wd_rdy[slot] = mark(f"WD{slot}")

        def ln_chunk(src, tc, stoff=0):
            slot = tc % 2
            tco = tc + stoff
            stg_v = stage[:, slot * H:(slot + 1) * H]
            op("sync", (lambda sl, r: lambda e: e.dma_start(
                out=x_tm[:, sl * H:(sl + 1) * H], in_=src[r:r + P]))(slot, tc * P),
               waits=xt_war[slot] + [store_marks[tc]], inc=(f"XT{slot}", 16))
            xt_rdy[slot] = mark(f"XT{slot}")
            op("vector", (lambda sl, t: lambda v: v.reduce_sum(
                st_s[:, t:t + 1], x_tm[:, sl * H:(sl + 1) * H], axis=AX.X))(slot, tco),
               waits=[xt_rdy[slot]], inc=("SV", 1))
            s_rdy = mark("SV")
            op("scalar", (lambda sl, t, sv: lambda s: s.activation(
                sv, x_tm[:, sl * H:(sl + 1) * H], AF.Square,
                accum_out=st_q[:, t:t + 1]))(slot, tco, stg_v),
               waits=[xt_rdy[slot]] + stage_war[slot], inc=("SA", 1))
            q_rdy = mark("SA")
            op("vector", (lambda t: lambda v: v.tensor_scalar_mul(
                st_mean[:, t:t + 1], st_s[:, t:t + 1], INV_H))(tco),
               waits=[s_rdy], inc=("SV", 1))
            op("vector", (lambda t: lambda v: v.tensor_mul(
                st_m2[:, t:t + 1], st_mean[:, t:t + 1], st_mean[:, t:t + 1]))(tco),
               waits=[mark("SV")], inc=("SV", 1))
            op("vector", (lambda t: lambda v: v.scalar_tensor_tensor(
                out=st_var[:, t:t + 1], in0=st_q[:, t:t + 1], scalar=INV_H,
                in1=st_m2[:, t:t + 1], op0=ALU.mult, op1=ALU.subtract))(tco),
               waits=[q_rdy, mark("SV")], inc=("SV", 1))
            var_rdy = mark("SV")
            op("scalar", (lambda t: lambda s: s.activation(
                st_ln[:, t:t + 1], st_var[:, t:t + 1], AF.Ln, bias=eps_t[:]))(tco),
               waits=[var_rdy, eps_rdy], inc=("SA", 1))
            op("scalar", (lambda t: lambda s: s.activation(
                st_rstd[:, t:t + 1], st_ln[:, t:t + 1], AF.Exp, scale=-0.5))(tco),
               waits=[mark("SA")], inc=("SA", 1))
            rstd_rdy = mark("SA")
            op("vector", (lambda t: lambda v: v.scalar_tensor_tensor(
                out=st_negmr[:, t:t + 1], in0=st_mean[:, t:t + 1], scalar=-1.0,
                in1=st_rstd[:, t:t + 1], op0=ALU.mult, op1=ALU.mult))(tco),
               waits=[rstd_rdy, mark("SV")], inc=("SV", 1))
            nm_rdy = mark("SV")
            op("scalar", (lambda sl, t, sv: lambda s: s.activation(
                sv, x_tm[:, sl * H:(sl + 1) * H], AF.Identity,
                bias=st_negmr[:, t:t + 1], scale=st_rstd[:, t:t + 1]))(slot, tco, stg_v),
               waits=[nm_rdy, mark("SA")], inc=("SA", 1))
            norm_rdy = mark("SA")
            xt_war[slot] = [norm_rdy, s_rdy]
            return norm_rdy

        def fm_chunk(tc, norm_rdy):
            slot = tc % 2
            evs = []
            for g_ in range(2):
                pti = state["pti"]
                for j in range(8):
                    c = 8 * g_ + j
                    waits = []
                    if j == 0:
                        waits = [norm_rdy, pt_war[pti], idn_rdy]
                    inc = ("SP", 1) if j == 7 else None
                    op("tensor", (lambda pi, sl, cc, jj: lambda t: t.transpose(
                        pt[pi][:, jj * 128:(jj + 1) * 128],
                        stage[:, sl * H + cc * 128: sl * H + (cc + 1) * 128],
                        idn[:]))(pti, slot, c, j),
                       waits=waits, inc=inc)
                tr_done = mark("SP")
                op("vector", (lambda pi, sl, g2: lambda v: v.tensor_copy(
                    x_fm_r[:, sl * H + g2 * 1024: sl * H + (g2 + 1) * 1024],
                    pt[pi][:, 0:1024]))(pti, slot, g_),
                   waits=[tr_done] + xfm_war[slot], inc=("SV", 1))
                pt_war[pti] = mark("SV")
                evs.append(mark("SV"))
                state["pti"] ^= 1
            stage_war[slot] = [tr_done]
            xfm_war[slot] = []
            return evs[-1], tr_done

        def route_chunk(tc, fm_rdy, sub_base):
            slot = tc % 2
            for c in range(HC):
                waits = []
                if c == 0:
                    waits = [fm_rdy, rw_rdy, pg_war[0]]
                inc = ("SP", 1) if c == HC - 1 else None
                op("tensor", (lambda sl, cc: lambda te: te.matmul(
                    pg[0][:, 0:E],
                    x_fm_r[:, sl * H + cc * 128: sl * H + (cc + 1) * 128],
                    rwT_sb[:, cc * E:(cc + 1) * E],
                    start=(cc == 0), stop=(cc == HC - 1)))(slot, c),
                   waits=waits, inc=inc)
            lg_rdy = mark("SP")
            xfm_war[slot].append(lg_rdy)
            op("vector", lambda v: v.tensor_copy(lg[:], pg[0][:, 0:E]),
               waits=[lg_rdy], inc=("SV", 1))
            pg_war[0] = mark("SV")
            op("vector", (lambda t: lambda v: v.reduce_max(
                st_nmax[:, t:t + 1], lg[:], axis=AX.X, negate=True))(tc),
               waits=[mark("SV"), mark("SA")], inc=("SV", 1))
            op("scalar", (lambda t: lambda s: s.activation(
                esb[:], lg[:], AF.Exp, bias=st_nmax[:, t:t + 1]))(tc),
               waits=[mark("SV"), esb_war[0]], inc=("SA", 1))
            e_rdy = mark("SA")
            op("vector", lambda v: v.max(out=m8[:], in_=esb[:]),
               waits=[e_rdy, mark("SV")], inc=("SV", 1))
            op("vector", lambda v: v.match_replace(
                out=zap[:], in_to_replace=m8[:], in_values=esb[:], imm_value=0.0),
               waits=[mark("SV")], inc=("SV", 1))
            op("vector", lambda v: v.tensor_sub(wsel[:], esb[:], zap[:]),
               waits=[mark("SV")], inc=("SV", 1))
            esb_war[0] = mark("SV")
            op("vector", (lambda t: lambda v: v.reduce_sum(
                st_rs[:, t:t + 1], wsel[:], axis=AX.X))(tc),
               waits=[mark("SV")], inc=("SV", 1))
            op("vector", (lambda t: lambda v: v.reciprocal(
                st_rr[:, t:t + 1], st_rs[:, t:t + 1]))(tc),
               waits=[mark("SV")], inc=("SV", 1))
            op("vector", (lambda t: lambda v: v.tensor_scalar_mul(
                w_sb[:], wsel[:], st_rr[:, t:t + 1]))(tc),
               waits=[mark("SV")], inc=("SV", 1))
            op("vector", lambda v: v.tensor_scalar(
                out=mask_f[:], in0=w_sb[:], scalar1=0.0, scalar2=None, op0=ALU.is_gt),
               waits=[mark("SV")], inc=("SV", 1))
            mf_rdy = mark("SV")
            op("vector", (lambda t: lambda v: v.tensor_copy(
                mask_b[:, t * E:(t + 1) * E], mask_f[:]))(tc),
               waits=[mf_rdy], inc=("SV", 1))
            mb_rdy = mark("SV")
            op("vector", lambda v: v.tensor_tensor_scan(
                out=incl32[:], data0=ones32[:], data1=mask_f[:],
                initial=100.0, op0=ALU.mult, op1=ALU.add),
               waits=[mf_rdy, ones_rdy], inc=("SV", 1))
            op("vector", lambda v: v.scalar_tensor_tensor(
                out=excl_m[:], in0=mask_f[:], scalar=-101.0, in1=incl32[:],
                op0=ALU.mult, op1=ALU.add),
               waits=[mark("SV")], inc=("SV", 1))
            excl_rdy = mark("SV")
            for i in range(tc + 1):
                waits = []
                if i == 0:
                    waits = [mb_rdy, pg_war[1], idn_rdy]
                inc = ("SP", 1) if i == tc else None
                lsl = (0 if i == tc else P)
                op("tensor", (lambda ii, ls: lambda te: te.matmul(
                    pg[1][:, 0:E],
                    ltb[:, ls:ls + P],
                    mask_b[:, ii * E:(ii + 1) * E],
                    start=(ii == 0), stop=(ii == tc)))(i, lsl),
                   waits=waits, inc=inc)
            pos_rdy = mark("SP")
            op("vector", lambda v: v.tensor_copy(pos_sb[:], pg[1][:, 0:E]),
               waits=[pos_rdy], inc=("SV", 1))
            pg_war[1] = mark("SV")
            op("vector", lambda v: v.tensor_add(slotid[:], pos_sb[:], ecc[:]),
               waits=[mark("SV")], inc=("SV", 1))
            sid_rdy = mark("SV")
            for k in range(K):
                op("vector", (lambda kk: lambda v: v.tensor_scalar(
                    out=eqk[:], in0=excl_m[:], scalar1=float(kk),
                    scalar2=None, op0=ALU.is_equal))(k),
                   waits=[excl_rdy, mark("SV")], inc=("SV", 1))
                op("vector", lambda v: v.tensor_mul(tmpk[:], eqk[:], slotid[:]),
                   waits=[sid_rdy, mark("SV")], inc=("SV", 1))
                op("vector", (lambda kk: lambda v: v.reduce_sum(
                    sk_f8[:, kk:kk + 1], tmpk[:], axis=AX.X))(k),
                   waits=[mark("SV")], inc=("SV", 1))
                op("vector", lambda v: v.tensor_mul(tmpk[:], eqk[:], w_sb[:]),
                   waits=[mark("SV")], inc=("SV", 1))
                op("vector", (lambda t, kk: lambda v: v.reduce_sum(
                    g_k[:, t * K + kk: t * K + kk + 1], tmpk[:], axis=AX.X))(tc, k),
                   waits=[mark("SV")], inc=("SV", 1))
            op("vector", (lambda t: lambda v: v.tensor_copy(
                sk_u[:, t * K:(t + 1) * K], sk_f8[:]))(tc),
               waits=[mark("SV")], inc=("SV", 1))
            sku_rdy = mark("SV")
            op("vector", (lambda t: lambda v: v.tensor_scalar_mul(
                g_k[:, t * K:(t + 1) * K], g_k[:, t * K:(t + 1) * K], 0.5))(tc),
               waits=[mark("SV")], inc=("SV", 1))
            for k in range(K):
                op("gpsimd", (lambda sl, t, kk, sbv: lambda g: g.indirect_dma_start(
                    out=xslots,
                    out_offset=IndirectOffsetOnAxis(
                        sk_u[:, t * K + kk: t * K + kk + 1], 0),
                    in_=sbv, in_offset=None))(
                        slot, tc, k, stage[:, slot * H:(slot + 1) * H]),
                   waits=[sku_rdy, sub_base], inc=("DS", 16))
                stage_war[slot] = stage_war[slot] + [mark("DS")]

        def tile_load(e_, st, ds_all):
            xslslot = state["xslslot"]
            state["xslslot"] ^= 1
            op("sync", (lambda ee, ss, xsl_s: lambda e: e.dma_start(
                out=xsl[:, xsl_s * H:(xsl_s + 1) * H],
                in_=xslots[ee * C + ss * P: ee * C + ss * P + P]))(e_, st, xslslot),
               waits=xsl_war[xslslot] + [ds_all], inc=("XS", 16))
            return xslslot, mark("XS")

        def emit_tile(e_, st, rdy, loaded):
            xb = e_ % 2
            xslslot, xsl_rdy = loaded
            tr_done = None
            for g_ in range(2):
                pti = state["pti"]
                for j in range(8):
                    c = 8 * g_ + j
                    waits = []
                    if j == 0:
                        waits = [xsl_rdy, pt_war[pti], idn_rdy]
                    inc = ("SP", 1) if j == 7 else None
                    op("tensor", (lambda pi, xs, cc, jj: lambda t: t.transpose(
                        pt[pi][:, jj * 128:(jj + 1) * 128],
                        xsl[:, xs * H + cc * 128: xs * H + (cc + 1) * 128],
                        idn[:]))(pti, xslslot, c, j),
                       waits=waits, inc=inc)
                tr_done = mark("SP")
                op("vector", (lambda pi, xv, g2, ss: lambda v: v.tensor_copy(
                    xv.rearrange("p (c s) -> p c s", c=HC)[
                        :, g2 * 8:(g2 + 1) * 8, ss * P:(ss + 1) * P],
                    pt[pi][:, 0:1024].rearrange("p (b t) -> p b t", b=8)))(
                        pti, xg[xb], g_, st),
                   waits=[tr_done] + xg_war[xb], inc=("SV", 1))
                pt_war[pti] = mark("SV")
                state["pti"] ^= 1
            xg_war[xb] = []
            xsl_war[xslslot] = [tr_done]
            rdy.append(mark("SV"))

        def attn_prep_step(ci, extra_waits):
            slot = ci % 3
            op("gpsimd", (lambda sl, c: lambda s: s.dma_start(
                out=wg_r[:, sl * 2048:(sl + 1) * 2048],
                in_=wTt[:, c * H:(c + 1) * H]))(slot, ci),
               waits=[wg_war[slot]], inc=(f"WG{slot}", 16))
            wg_rdy[slot] = mark(f"WG{slot}")
            op("gpsimd", (lambda sl, c: lambda s: s.dma_start(
                out=wu_r[:, sl * 2048:(sl + 1) * 2048],
                in_=dwTt[:, c * H:(c + 1) * H]))(slot, ci),
               waits=[wu_war[slot]], inc=(f"WU{slot}", 16))
            wu_rdy[slot] = mark(f"WU{slot}")
            op("gpsimd", (lambda sl, c: lambda g: g.tensor_add(
                wsumT[:, c * H:(c + 1) * H],
                wg_r[:, sl * 2048:(sl + 1) * 2048],
                wu_r[:, sl * 2048:(sl + 1) * 2048]))(slot, ci),
               waits=[wg_rdy[slot], wu_rdy[slot]] + extra_waits,
               inc=("SG", 1))
            wg_war[slot] = mark("SG")
            wu_war[slot] = mark("SG")
            if ci == HC - 1:
                state["wsum_rdy"] = mark("SG")

        def experts(sweep, emit_prep=False):
            ds_all = mark("DS")
            load_wg(0, 0, sweep * 7919 + 0)
            load_wu(0, 0, sweep * 7919 + 0)
            load_wg(0, 1, sweep * 7919 + 1)
            load_wu(0, 1, sweep * 7919 + 1)
            load_wd(0, 0, sweep * 7919 + 0)
            tiles_rdy = []
            for st in range(ST):
                emit_tile(0, st, tiles_rdy, tile_load(0, st, ds_all))
            for e_ in range(E):
                xb = e_ % 2
                tile_rdy = tiles_rdy
                h_last = None
                for m in range(IC):
                    seq = sweep * 7919 + e_ * IC + m
                    slot = seq % 3
                    wgr = wg_rdy[slot]
                    wur = wu_rdy[slot]
                    g_done = None
                    u_done = None
                    for n in range(NSUB):
                        xg_rdy = tile_rdy[((n + 1) * TSUB + P - 1) // P - 1]
                        gi = state["gi"]
                        g_b, u_b = pg[gi], pu[gi]
                        for c in range(HC):
                            waits = []
                            if c == 0:
                                waits = [wgr, xg_rdy, pg_war[gi]]
                            inc = ("SP", 1) if c == HC - 1 else None
                            op("tensor", (lambda gb, sl, cc, nn, xv: lambda t: t.matmul(
                                gb[:, 0:TSUB],
                                wg_r[:, sl * 2048 + cc * 128: sl * 2048 + cc * 128 + 128],
                                xv[:, cc * C + nn * TSUB: cc * C + (nn + 1) * TSUB],
                                start=(cc == 0), stop=(cc == HC - 1)))(g_b, slot, c, n, xg[xb]),
                               waits=waits, inc=inc)
                        g_done = mark("SP")
                        for c in range(HC):
                            waits = []
                            if c == 0:
                                waits = [wur, xg_rdy, pu_war[gi]]
                            inc = ("SP", 1) if c == HC - 1 else None
                            op("tensor", (lambda ub, sl, cc, nn, xv: lambda t: t.matmul(
                                ub[:, 0:TSUB],
                                wu_r[:, sl * 2048 + cc * 128: sl * 2048 + cc * 128 + 128],
                                xv[:, cc * C + nn * TSUB: cc * C + (nn + 1) * TSUB],
                                start=(cc == 0), stop=(cc == HC - 1)))(u_b, slot, c, n, xg[xb]),
                               waits=waits, inc=inc)
                        u_done = mark("SP")
                        sgv = sg[:, gi * TSUB: (gi + 1) * TSUB]
                        op("scalar", (lambda gb, sv: lambda s: s.activation(
                            sv, gb[:, 0:TSUB], AF.Silu))(g_b, sgv),
                           waits=[g_done, sg_war[gi]], inc=("SA", 1))
                        pg_war[gi] = mark("SA")
                        silu_done = mark("SA")
                        hv = h_sl[:, m * C + n * TSUB: m * C + (n + 1) * TSUB]
                        op("vector", (lambda ub, sv, hh: lambda v: v.tensor_mul(
                            hh, sv, ub[:, 0:TSUB]))(u_b, sgv, hv),
                           waits=[u_done, silu_done, h_war[0]], inc=("SV", 1))
                        pu_war[gi] = mark("SV")
                        sg_war[gi] = mark("SV")
                        h_last = mark("SV")
                        state["gi"] ^= 1
                    wg_war[slot] = g_done
                    wu_war[slot] = u_done
                    nm, ne_ = m + 2, e_
                    if nm >= IC:
                        nm -= IC
                        ne_ += 1
                    if ne_ < E:
                        load_wg(ne_, nm, sweep * 7919 + ne_ * IC + nm)
                        load_wu(ne_, nm, sweep * 7919 + ne_ * IC + nm)
                xg_war[xb] = [mark("SP")]
                big_attn_war[0] = big_attn_war[0] + [mark("SP")]
                tiles_rdy = []
                if e_ + 1 < E:
                    state["pend_tile"] = (0, tile_load(e_ + 1, 0, ds_all))
                for st in range(ST):
                    osl = st % 2
                    q_done = []
                    for q in range(NQ):
                        seq = sweep * 7919 + e_ * NQ + q
                        slot = seq % 6
                        wdr = wd_rdy[slot]
                        if st == 0:
                            nq_, ne_ = q + 1, e_
                            if nq_ >= NQ:
                                nq_ -= NQ
                                ne_ += 1
                            if ne_ < E:
                                load_wd(ne_, nq_, sweep * 7919 + ne_ * NQ + nq_)
                        di = state["di"]
                        d_b = pd[di]
                        for c in range(IC):
                            waits = []
                            if c == 0:
                                waits = [wdr, h_last, pd_war[di]]
                            inc = ("SP", 1) if c == IC - 1 else None
                            op("tensor", (lambda db, sl, cc, qq, ss: lambda t: t.matmul(
                                db[:],
                                h_sl[:, cc * C + ss * P: cc * C + (ss + 1) * P],
                                wd_r[:, sl * 3072 + cc * 512: sl * 3072 + (cc + 1) * 512],
                                start=(cc == 0), stop=(cc == IC - 1)))(d_b, slot, c, q, st),
                               waits=waits, inc=inc)
                        d_done = mark("SP")
                        if st == ST - 1:
                            wd_war[slot] = d_done
                        op("vector", (lambda db, ov, qq: lambda v: v.tensor_copy(
                            ov[:, qq * 512:(qq + 1) * 512], db[:]))(d_b, o_sl[osl], q),
                           waits=[d_done] + osl_war[osl], inc=("SV", 1))
                        pd_war[di] = mark("SV")
                        q_done.append(mark("SV"))
                        state["di"] ^= 1
                    osl_war[osl] = []
                    op("sync", (lambda ov, ee, ss: lambda e: e.dma_start(
                        out=dall[ee * C + ss * P: ee * C + ss * P + P], in_=ov))(
                            o_sl[osl], e_, st),
                       waits=[q_done[-1], state["dg_all"]], inc=("SD", 16))
                    osl_war[osl] = [mark("SD")]
                    if e_ + 1 < E:
                        pst_, loaded = state["pend_tile"]
                        emit_tile(e_ + 1, pst_, tiles_rdy, loaded)
                        if pst_ + 1 < ST:
                            state["pend_tile"] = (pst_ + 1,
                                                  tile_load(e_ + 1, pst_ + 1, ds_all))
                h_war[0] = mark("SP")
            if emit_prep:
                late = [h_war[0], mark("SD")]
                for ci in range(HC):
                    lo, hi = ci * H, (ci + 1) * H
                    if hi <= XGW:
                        extra = list(xg_war[0])
                    elif lo >= XGW and hi <= 2 * XGW:
                        extra = list(xg_war[1])
                    elif lo >= XGW:
                        extra = list(xg_war[1]) + late
                    else:
                        extra = list(xg_war[0]) + list(xg_war[1]) + late
                    attn_prep_step(ci, extra)
            state["sd_all"] = mark("SD")
            state["xs_all"] = mark("XS")

        def combine_chunk(src, dst, tc, sd_all, cbufs=2):
            slot = tc % 2
            av = o_t[:, slot * H:(slot + 1) * H]
            op("sync", (lambda sl, r, aa: lambda e: e.dma_start(
                out=aa, in_=src[r:r + P]))(slot, tc * P, av),
               waits=[o_war[slot], store_marks[tc]], inc=(f"XT{slot}", 16))
            x_rdy = mark(f"XT{slot}")
            last_acc = None
            for k in range(K):
                cb = (tc * K + k) % cbufs
                cbuf = comb[cb] if cb < 2 else x_fm_r[:, (cb - 2) * H:(cb - 1) * H]
                op("gpsimd", (lambda t, kk, cv: lambda g: g.indirect_dma_start(
                    out=cv, out_offset=None,
                    in_=dall,
                    in_offset=IndirectOffsetOnAxis(
                        sk_u[:, t * K + kk: t * K + kk + 1], 0)))(tc, k, cbuf),
                   waits=[sd_all] + comb_war[cb], inc=("DG", 16))
                g_rdy = mark("DG")
                gv = g_k[:, tc * K + k: tc * K + k + 1]
                op("vector", (lambda cv, aa, gg: lambda v: v.scalar_tensor_tensor(
                    out=aa, in0=cv, scalar=gg, in1=aa,
                    op0=ALU.mult, op1=ALU.add))(cbuf, av, gv),
                   waits=[g_rdy] + ([x_rdy] if k == 0 else []), inc=("SV", 1))
                comb_war[cb] = [mark("SV")]
                last_acc = mark("SV")
            op("sync", (lambda sl, r, aa: lambda e: e.dma_start(
                out=dst[r:r + P], in_=aa))(slot, tc * P, av),
               waits=[last_acc], inc=("SO", 16))
            o_war[slot] = mark("SO")
            store_marks[tc] = mark("SO")

        def ffnA_chunk(src, tc, sub_base):
            norm_rdy = ln_chunk(src, tc)
            fm_rdy, _ = fm_chunk(tc, norm_rdy)
            route_chunk(tc, fm_rdy, sub_base)

        def attn_chunk(tc):
            wsum_rdy = state["wsum_rdy"]
            slot = tc % 2
            norm_rdy = ln_chunk(stream, tc, stoff=NC_TOK)
            fm_rdy, _ = fm_chunk(tc, norm_rdy)
            for no in range(NQ):
                di = state["di"]
                d_b = pd[di]
                for ci in range(HC):
                    waits = []
                    if ci == 0:
                        waits = [wsum_rdy, fm_rdy, pd_war[di]]
                    inc = ("SP", 1) if ci == HC - 1 else None
                    op("tensor", (lambda db, sl, cc, nn: lambda te: te.matmul(
                        db[:],
                        x_fm_r[:, sl * H + cc * 128: sl * H + (cc + 1) * 128],
                        wsumT[:, cc * H + nn * 512: cc * H + (nn + 1) * 512],
                        start=(cc == 0), stop=(cc == HC - 1)))(d_b, slot, ci, no),
                       waits=waits, inc=inc)
                d_done = mark("SP")
                if no == NQ - 1:
                    xfm_war[slot].append(d_done)
                op("vector", (lambda db, sl, nn: lambda v: v.tensor_add(
                    x_tm[:, sl * H + nn * 512: sl * H + (nn + 1) * 512], db[:],
                    x_tm[:, sl * H + nn * 512: sl * H + (nn + 1) * 512]))(d_b, slot, no),
                   waits=[d_done], inc=("SV", 1))
                pd_war[di] = mark("SV")
                state["di"] ^= 1
            ev_done = mark("SV")
            op("sync", (lambda sl, r: lambda e: e.dma_start(
                out=stream[r:r + P], in_=x_tm[:, sl * H:(sl + 1) * H]))(slot, tc * P),
               waits=[ev_done], inc=("SO", 16))
            xt_war[slot] = [mark("SO")]
            store_marks[tc] = mark("SO")

        for tc in range(NC_TOK):
            ffnA_chunk(x_in, tc, None)
        experts(0, emit_prep=True)
        sd_all = state["sd_all"]
        sub2 = state["xs_all"]
        for step in range(NC_TOK + 1):
            if step < NC_TOK:
                combine_chunk(x_in, stream, step, sd_all)
            if step >= 1:
                attn_chunk(step - 1)
        for tc in range(NC_TOK):
            ffnA_chunk(stream, tc, sub2)
        state["dg_all"] = mark("DG")
        m = mark("SP")
        xg_war[0] = xg_war[0] + [m]
        xg_war[1] = xg_war[1] + [m]
        h_war[0] = m
        experts(1)
        sd_all = state["sd_all"]
        xfm_free = xfm_war[0] + xfm_war[1]
        comb_war[2] = comb_war[2] + xfm_free
        comb_war[3] = comb_war[3] + xfm_free
        for tc in range(NC_TOK):
            combine_chunk(stream, y_out, tc, sd_all, cbufs=4)

        total_so = cnt["SO"]
        op("sync", lambda e: None, waits=[("SO", total_so)])

        def emit(engine_name):
            def run(eng):
                waited = {}
                for (en, fn, ws, incs) in sched:
                    if en != engine_name:
                        continue
                    for (s, v) in ws:
                        if v > waited.get(s, 0):
                            eng.wait_ge(sems[s], v)
                            waited[s] = v
                    ins = fn(eng)
                    for (s, n) in incs:
                        ins.then_inc(sems[s], n)
            return run

        block.vector(emit("vector"))
        block.scalar(emit("scalar"))
        block.tensor(emit("tensor"))
        block.gpsimd(emit("gpsimd"))
        block.sync(emit("sync"))
    return nc


def _tile_inputs(L0_attn_W, L0_attn_delta_W, L0_ffn_router_w, L0_ffn_w_gate,
                 L0_ffn_w_up, L0_ffn_w_down):
    f32 = np.float32
    wg = np.asarray(L0_ffn_w_gate, dtype=f32)
    wu = np.asarray(L0_ffn_w_up, dtype=f32)
    wd = np.asarray(L0_ffn_w_down, dtype=f32)
    wg_t = np.ascontiguousarray(
        wg.reshape(E, HC, P, IC, 128).transpose(0, 3, 2, 1, 4).reshape(E, IC, P, HC * 128))
    wu_t = np.ascontiguousarray(
        wu.reshape(E, HC, P, IC, 128).transpose(0, 3, 2, 1, 4).reshape(E, IC, P, HC * 128))
    wd_t = np.ascontiguousarray(
        wd.reshape(E, IC, P, NQ, 512).transpose(0, 3, 2, 1, 4).reshape(E, NQ, P, IC * 512))
    WT = np.ascontiguousarray(np.asarray(L0_attn_W, dtype=f32).T)
    dWT = np.ascontiguousarray(np.asarray(L0_attn_delta_W, dtype=f32).T)
    wTt = np.ascontiguousarray(WT.reshape(HC, P, H).transpose(1, 0, 2).reshape(P, HC * H))
    dwTt = np.ascontiguousarray(dWT.reshape(HC, P, H).transpose(1, 0, 2).reshape(P, HC * H))
    rwT = np.ascontiguousarray(np.asarray(L0_ffn_router_w, dtype=f32).T)
    rwTt = np.ascontiguousarray(rwT.reshape(HC, P, E).transpose(1, 0, 2).reshape(P, HC * E))
    idn = np.eye(P, dtype=np.float32).astype(ml_dtypes.bfloat16)
    lt = np.triu(np.ones((P, P), dtype=np.float32), 1)
    lt_in = np.concatenate([lt, np.ones((P, P), dtype=np.float32)],
                           axis=1).astype(ml_dtypes.bfloat16)
    ecc_in = np.broadcast_to(
        (np.arange(E, dtype=np.float32) * C)[None, :], (P, E)).copy()
    return dict(wg_t=wg_t, wu_t=wu_t, wd_t=wd_t, wTt=wTt, dwTt=dwTt,
                rwTt=rwTt, idn_in=idn, lt_in=lt_in, ecc_in=ecc_in)


def kernel(L0_attn_x, L0_attn_W, L0_attn_target_W, L0_attn_delta_W,
           L0_ffn_router_w, L0_ffn_w_gate, L0_ffn_w_up, L0_ffn_w_down):
    assert np.asarray(L0_attn_target_W).shape == (H, H)
    x = np.ascontiguousarray(np.asarray(L0_attn_x, dtype=np.float32).reshape(16384, H))
    shared = _tile_inputs(L0_attn_W, L0_attn_delta_W, L0_ffn_router_w,
                          L0_ffn_w_gate, L0_ffn_w_up, L0_ffn_w_down)
    nc = build_nc()
    in_maps = [dict(shared, x_in=np.ascontiguousarray(x[c * NT:(c + 1) * NT]))
               for c in range(NCORES)]
    res = run_bass_kernel_spmd(nc, in_maps, core_ids=list(range(NCORES)))
    y = np.concatenate([np.asarray(res.results[c]["y"]) for c in range(NCORES)], axis=0)
    return y.reshape(4, 4096, H).astype(np.float32)
